# version 19
# baseline (speedup 1.0000x reference)
import numpy as np
import ml_dtypes
from contextlib import ExitStack

import concourse.bass as bass
import concourse.mybir as mybir
from concourse.bass_utils import run_bass_kernel_spmd
from concourse.bass_core_rust import IndirectOffsetOnAxis

F32 = mybir.dt.float32
BF16 = mybir.dt.bfloat16
U32 = mybir.dt.uint32
AF = mybir.ActivationFunctionType
ALU = mybir.AluOpType
AX = mybir.AxisListType.X

T = 1024
NT = 8
DM = 4096
FOUT = 3072
W = 1024
H = 16
MID = 1024
XP = 1056
XLNR_COLS = 32 * XP


def ap3(t, off, dims):
    free = int(np.prod(t.shape[1:]))
    return bass.AP(t, off, [[free, 128]] + dims)


def build_nc(taps=(), last_phase=8, p6dbg='', n6=8):
    taps = set(taps)
    nc = bass.Bass("TRN2", target_bir_lowering=False)

    x_in = nc.dram_tensor("x_in", [T + 2, DM], F32, kind="ExternalInput").ap()
    convw_d = nc.dram_tensor("convw", [3, DM, FOUT], F32, kind="ExternalInput").ap()
    convb_d = nc.dram_tensor("convb", [FOUT], F32, kind="ExternalInput").ap()
    wq_d = nc.dram_tensor("wq", [W, W], F32, kind="ExternalInput").ap()
    wk_d = nc.dram_tensor("wk", [W, W], F32, kind="ExternalInput").ap()
    wv_d = nc.dram_tensor("wv", [W, W], F32, kind="ExternalInput").ap()
    bq2_d = nc.dram_tensor("bq2d", [128, 8], F32, kind="ExternalInput").ap()
    bk2_d = nc.dram_tensor("bk2d", [128, 8], F32, kind="ExternalInput").ap()
    bv_d = nc.dram_tensor("bv", [W], F32, kind="ExternalInput").ap()
    wo_d = nc.dram_tensor("wo", [W, W], F32, kind="ExternalInput").ap()
    bo2_d = nc.dram_tensor("bo2d", [128, 8], F32, kind="ExternalInput").ap()
    projw_d = nc.dram_tensor("projw", [W, DM], F32, kind="ExternalInput").ap()
    projb_d = nc.dram_tensor("projb", [DM], F32, kind="ExternalInput").ap()
    up_d = nc.dram_tensor("ffup", [H, 256, MID], F32, kind="ExternalInput").ap()
    down_d = nc.dram_tensor("ffdown", [H, MID, 256], F32, kind="ExternalInput").ap()
    perm_d = nc.dram_tensor("perm2d", [128, 32], U32, kind="ExternalInput").ap()
    mask_d = nc.dram_tensor("mask4d", [128, 512], F32, kind="ExternalInput").ap()
    out_d = nc.dram_tensor("out", [T, DM], F32, kind="ExternalOutput").ap()

    xln2_d = nc.dram_tensor("xln2fm", [DM, T], BF16, kind="Internal").ap()
    scat_d = nc.dram_tensor("scat", [DM, T], BF16, kind="Internal").ap()

    tap_specs = {
        "t_stage": [128, 24576],
        "t_xln1": [128, XLNR_COLS],
        "t_conv": [128, 24576],
        "t_qkvh": [128, 24576],
        "t_attnw": [128, 8192],
        "t_ofm": [128, 8192],
        "t_pall": [128, 8192],
        "t_x1a": [128, 32768],
        "t_x1": [128, 32768],
    }
    tap_out = {n: nc.dram_tensor(n, s, BF16, kind="ExternalOutput").ap()
               for n, s in tap_specs.items() if n in taps}
    if "t_stt" in taps:
        tap_out["t_stt"] = nc.dram_tensor("t_stt", [128, 16], F32, kind="ExternalOutput").ap()
    if "t_sm" in taps:
        tap_out["t_sm"] = nc.dram_tensor("t_sm", [128, 1024], F32, kind="ExternalOutput").ap()
        tap_out["t_sm2"] = nc.dram_tensor("t_sm2", [128, 8], F32, kind="ExternalOutput").ap()
        tap_out["t_sm3"] = nc.dram_tensor("t_sm3", [128, 8], F32, kind="ExternalOutput").ap()
        tap_out["t_sm5"] = nc.dram_tensor("t_sm5", [128, 8], F32, kind="ExternalOutput").ap()
    if "t_x0" in taps:
        tap_out["t_x0"] = nc.dram_tensor("t_x0", [128, 8192], F32, kind="ExternalOutput").ap()
    for n, src in (("t_xln2", xln2_d), ("t_scat", scat_d)):
        if n in taps:
            tap_out[n] = nc.dram_tensor(n, [DM, T], BF16, kind="ExternalOutput").ap()

    with ExitStack() as ctx:
        sb = lambda n, s, d: ctx.enter_context(nc.sbuf_tensor(n, s, d))
        XLNR = sb("XLNR", [128, XLNR_COLS], BF16)
        QKVR = sb("QKVR", [128, 24576], BF16)
        ATTNW = sb("ATTNW", [128, 8192], BF16)
        ARENA = sb("ARENA", [128, 49152], mybir.dt.uint8)
        A16 = ARENA.bitcast(BF16)
        A32 = ARENA.bitcast(F32)
        SLAB = sb("SLAB", [128, 4096], BF16)
        ones_r = sb("ones_r", [1, 512], BF16)
        convb_r = ATTNW[0:1, 0:FOUT]
        bv_r = ATTNW[0:1, FOUT:FOUT + W]
        projb_r = QKVR[0:1, 0:DM]
        bq2 = sb("bq2", [128, 8], F32)
        bk2 = sb("bk2", [128, 8], F32)
        bo2 = sb("bo2", [128, 8], F32)
        mask4 = sb("mask4", [128, 512], F32)
        perm2 = sb("perm2", [128, 32], U32)
        epsc = sb("epsc", [128, 1], F32)
        neg16 = sb("neg16", [128, 1], F32)
        halo_fm = sb("halo_fm", [128, 512], BF16)
        s_m4 = sb("s_m4", [128, 1024], F32)
        negmax4 = sb("negmax4", [128, 8], F32)
        sums4 = sb("sums4", [128, 8], F32)
        rsum4 = sb("rsum4", [128, 8], F32)
        stt = sb("stt", [128, 16], F32)
        PS = ctx.enter_context(nc.psum_tensor("PS", [128, 4096], F32))

        qS = ctx.enter_context(nc.semaphore("qS"))
        qG = ctx.enter_context(nc.semaphore("qG"))
        qT = ctx.enter_context(nc.semaphore("qT"))
        qD = ctx.enter_context(nc.semaphore("qD"))
        sPE = ctx.enter_context(nc.semaphore("sPE"))
        sPL = ctx.enter_context(nc.semaphore("sPL"))
        sDV = ctx.enter_context(nc.semaphore("sDV"))
        sAC = ctx.enter_context(nc.semaphore("sAC"))

        dump_cnt = [0]

        def _dump(dst, src_ap):
            with nc.Block() as dblk:
                @dblk.sync
                def _(s):
                    s.dma_start(dst, src_ap).then_inc(qD, 16)
                    dump_cnt[0] += 16
                    s.wait_ge(qD, dump_cnt[0])

        def _dump_dram(dst, src_dram):
            with nc.Block() as dblk:
                @dblk.sync
                def _(s):
                    for c in range(32):
                        s.dma_start(SLAB[:, 0:1024], src_dram[c * 128:(c + 1) * 128, :]).then_inc(qD, 16)
                        dump_cnt[0] += 16
                        s.wait_ge(qD, dump_cnt[0])
                        s.dma_start(dst[c * 128:(c + 1) * 128, :], SLAB[:, 0:1024]).then_inc(qD, 16)
                        dump_cnt[0] += 16
                        s.wait_ge(qD, dump_cnt[0])

        def bank(b, wd=512, off=0, p0=0, p1=128):
            return PS[p0:p1, b * 512 + off: b * 512 + off + wd]

        def slab_buf(j):
            sl = j % 4
            if sl < 2:
                return SLAB[:, sl * 2048: sl * 2048 + 2048]
            return A16[:, (sl - 2) * 2048:(sl - 2) * 2048 + 2048]

        def xln_rhs(k, w, th):
            c = k * XP + 15 + w + th * 512
            return XLNR[:, c: c + 512]

        def x1v(t_, c0, wd):
            return XLNR[:, t_ * 4096 + c0: t_ * 4096 + c0 + wd]

        def x0st(i):
            return A32[:, (i % 2) * DM:(i % 2) * DM + DM]

        def x0b(t_):
            if t_ < 4:
                off = 8192 + t_ * 4096
            elif t_ < 6:
                off = (t_ - 4) * 4096
            else:
                off = 8192 + (t_ - 6) * 4096
            return A16[:, off: off + 4096]

        def xt_st(i):
            return A16[:, 16384 + (i % 2) * DM: 16384 + (i % 2) * DM + DM]

        def st(i, c):
            return stt[:, (i % 2) * 8 + c:(i % 2) * 8 + c + 1]

        def sth(c):
            return stt[0:2, 8 + c: 8 + c + 1]

        qs_aux = 5 * 16
        qs_x0e = lambda i: qs_aux + 16 * (i // 2 + 1)
        qs_p1_end = qs_aux + 16 * 6
        qg_x0 = {1: 3 * 16, 3: 4 * 16, 5: 7 * 16, 7: 8 * 16}
        slab_val = lambda j: 16 * (5 + j) if j < 2 else 16 * (8 + j - 1)
        dv_sum = lambda i: 5 * i + 1
        dv_nm = lambda i: 5 * i + 2
        dv_t1 = lambda i: 5 * i + 3
        dv_pre = lambda i: 5 * i + 4
        dv_apply = lambda i: 5 * i + 5
        dv_sum_h, dv_nm_h, dv_t1_h, dv_pre_h, dv_apply_h, dv_halo = 41, 42, 43, 44, 45, 46
        ac_sq = lambda i: 3 * i + 1
        ac_ln = lambda i: 3 * i + 2
        ac_rstd = lambda i: 3 * i + 3
        ac_sq_h, ac_ln_h, ac_rstd_h = 25, 26, 27
        qt_tp1 = lambda i: 16 * (i + 1)
        qt_tp1h = 16 * 9

        with nc.Block("p1") as blk:
            @blk.sync
            def _(s):
                s.dma_start(bq2[:, :], bq2_d).then_inc(qS, 16)
                s.dma_start(bk2[:, :], bk2_d).then_inc(qS, 16)
                s.dma_start(bo2[:, :], bo2_d).then_inc(qS, 16)
                s.dma_start(mask4[:, :], mask_d).then_inc(qS, 16)
                s.dma_start(perm2[:, :], perm_d).then_inc(qS, 16)
                for i in range(0, NT, 2):
                    if i >= 2:
                        s.wait_ge(sDV, dv_apply(i - 2))
                    s.dma_start(x0st(i), x_in[1 + 128 * i: 129 + 128 * i, :]).then_inc(qS, 16)
                s.wait_ge(sDV, dv_apply(6))
                s.dma_start(A32[0:1, 0:DM], x_in[0:1, :]).then_inc(qS, 16)
                s.dma_start(A32[1:2, 0:DM], x_in[T + 1:T + 2, :]).then_inc(qS, 16)

            @blk.gpsimd
            def _(g):
                g.dma_start(convb_r, convb_d[None, :]).then_inc(qG, 16)
                g.dma_start(bv_r, bv_d[None, :]).then_inc(qG, 16)
                g.dma_start(x0st(1), x_in[129:257, :]).then_inc(qG, 16)
                g.wait_ge(sDV, dv_apply(1))
                g.dma_start(x0st(3), x_in[1 + 384: 129 + 384, :]).then_inc(qG, 16)
                for j in range(2):
                    src = bass.AP(convw_d.tensor, (j * 512) * FOUT,
                                  [[FOUT, 128], [128 * FOUT, 4], [1, 512]])
                    dst = ap3(slab_buf(j).tensor, slab_buf(j).offset, [[512, 4], [1, 512]])
                    g.dma_start(dst, src).then_inc(qG, 16)
                g.wait_ge(sDV, dv_apply(3))
                g.dma_start(x0st(5), x_in[1 + 640: 129 + 640, :]).then_inc(qG, 16)
                g.wait_ge(sDV, dv_apply(5))
                g.dma_start(x0st(7), x_in[1 + 896: 129 + 896, :]).then_inc(qG, 16)



            @blk.vector
            def _(v):
                v.memset(ones_r[:, :], 1.0)
                v.memset(epsc[:, :], 1e-5)
                v.memset(neg16[:, :], -16.0)
                for i in range(NT):
                    if i % 2 == 0:
                        v.wait_ge(qS, qs_x0e(i))
                    else:
                        v.wait_ge(qG, qg_x0[i])
                    v.reduce_sum(st(i, 0), x0st(i), axis=AX).then_inc(sDV, 1)
                    v.wait_ge(sDV, dv_sum(i))
                    v.tensor_scalar(st(i, 2), st(i, 0), -1.0 / DM, None, ALU.mult).then_inc(sDV, 1)
                    v.wait_ge(sAC, ac_sq(i))
                    v.wait_ge(sDV, dv_nm(i))
                    v.tensor_tensor(st(i, 3), st(i, 2), st(i, 2), ALU.mult)
                    v.tensor_scalar(st(i, 4), st(i, 1), 1.0 / DM, None, ALU.mult).then_inc(sDV, 1)
                    v.wait_ge(sDV, dv_t1(i))
                    v.tensor_tensor(st(i, 6), st(i, 4), st(i, 3), ALU.subtract).then_inc(sDV, 1)
                    v.wait_ge(sAC, ac_rstd(i))
                    if i >= 2:
                        v.wait_ge(qT, qt_tp1(i - 2))
                    v.tensor_scalar(xt_st(i), x0st(i), st(i, 2), st(i, 5),
                                    ALU.add, ALU.mult).then_inc(sDV, 1)
                x0h = A32[0:2, 0:DM]
                xth = A16[0:16, 16384:16384 + DM]
                v.wait_ge(qS, qs_p1_end)
                v.reduce_sum(sth(0), A32[0:2, 0:DM], axis=AX).then_inc(sDV, 1)
                v.wait_ge(sDV, dv_sum_h)
                v.tensor_scalar(sth(2), sth(0), -1.0 / DM, None, ALU.mult).then_inc(sDV, 1)
                v.wait_ge(sAC, ac_sq_h)
                v.wait_ge(sDV, dv_nm_h)
                v.tensor_tensor(sth(3), sth(2), sth(2), ALU.mult)
                v.tensor_scalar(sth(4), sth(1), 1.0 / DM, None, ALU.mult).then_inc(sDV, 1)
                v.wait_ge(sDV, dv_t1_h)
                v.tensor_tensor(sth(6), sth(4), sth(3), ALU.subtract).then_inc(sDV, 1)
                v.wait_ge(sAC, ac_rstd_h)
                v.wait_ge(qT, qt_tp1(6))
                v.tensor_scalar(xth[0:2, :], x0h, sth(2), sth(5),
                                ALU.add, ALU.mult).then_inc(sDV, 1)
                v.wait_ge(qT, qt_tp1h)
                v.tensor_copy(bass.AP(XLNR, 15, [[XLNR_COLS, 128], [XP, 32], [1, 1]]),
                              bass.AP(A16, 20480, [[24576, 128], [128, 32], [1, 1]]))
                v.tensor_copy(bass.AP(XLNR, 1040, [[XLNR_COLS, 128], [XP, 32], [1, 1]]),
                              bass.AP(A16, 20481, [[24576, 128], [128, 32], [1, 1]])).then_inc(sDV, 1)

            @blk.scalar
            def _(sc):
                for i in range(NT):
                    if i % 2 == 0:
                        sc.wait_ge(qS, qs_x0e(i))
                    else:
                        sc.wait_ge(qG, qg_x0[i])
                    sc.activation(QKVR[:, 0:4096], x0st(i), AF.Square,
                                  accum_out=st(i, 1)).then_inc(sAC, 1)
                    sc.wait_ge(sDV, dv_pre(i))
                    sc.activation(st(i, 5), st(i, 6), AF.Ln, bias=epsc[:, :]).then_inc(sAC, 1)
                    sc.wait_ge(sAC, ac_ln(i))
                    sc.activation(st(i, 5), st(i, 5), AF.Exp, scale=-0.5).then_inc(sAC, 1)
                    sc.wait_ge(sDV, dv_apply(i))
                    dst = bass.AP(XLNR, 16 + 128 * i, [[XLNR_COLS, 128], [XP, 32], [1, 128]])
                    sc.dma_start(dst, xt_st(i), transpose=True).then_inc(qT, 16)
                x0h = A32[0:2, 0:DM]
                xth16 = A16[0:16, 16384:16384 + DM]
                sc.wait_ge(qS, qs_p1_end)
                sc.activation(QKVR[0:2, 0:4096], x0h, AF.Square,
                              accum_out=sth(1)).then_inc(sAC, 1)
                sc.wait_ge(sDV, dv_pre_h)
                sc.activation(sth(5), sth(6), AF.Ln, bias=epsc[0:2, :]).then_inc(sAC, 1)
                sc.wait_ge(sAC, ac_ln_h)
                sc.activation(sth(5), sth(5), AF.Exp, scale=-0.5).then_inc(sAC, 1)
                sc.wait_ge(sDV, dv_apply_h)
                sc.wait_ge(qT, qt_tp1(7))
                dsth = bass.AP(A16, 20480, [[24576, 128], [128, 32], [1, 128]])
                sc.dma_start(dsth, A16[:, 16384:16384 + DM], transpose=True).then_inc(qT, 16)

        if "t_xln1" in tap_out:
            _dump(tap_out["t_xln1"], XLNR[:, :])
        if "t_stage" in tap_out:
            _dump(tap_out["t_stage"], A16[:, 0:24576])
        if "t_stt" in tap_out:
            _dump(tap_out["t_stt"], stt[:, :])
        if "t_x0" in tap_out:
            _dump(tap_out["t_x0"], A32[:, 0:8192])

        if last_phase < 2:
            return nc
        pe_slab = lambda j: (j + 1) + (j // 24) + 1
        pe_stop = lambda g: 25 * (g + 1)
        dv2_evac = lambda g: dv_halo + 4 * (g + 1)
        ac2_0 = 3 * NT + 3
        ac2_evac = lambda g: ac2_0 + 4 * (g + 1)
        wqkv_val = lambda x: slab_val(143) + 16 * (x + 1)

        with nc.Block("p2") as blk:
            @blk.tensor
            def _(t):
                t.wait_ge(qT, qt_tp1(7))
                t.wait_ge(sDV, dv_halo)
                j = 0
                for g in range(6):
                    if g >= 1:
                        t.wait_ge(sDV, dv2_evac(g - 1))
                        t.wait_ge(sAC, ac2_evac(g - 1))
                    for mt in range(4):
                        lhsT = ATTNW[0:1, (4 * g + mt) * 128:(4 * g + mt) * 128 + 128]
                        for th in range(2):
                            ins = t.matmul(bank(mt * 2 + th), lhsT, ones_r[0:1, 0:512],
                                           start=True, stop=False)
                    ins.then_inc(sPE, 1)
                    for w in range(3):
                        for k4 in range(8):
                            t.wait_ge(qG, slab_val(j))
                            slab = slab_buf(j)
                            last = (w == 2 and k4 == 7)
                            for kk in range(4):
                                k = k4 * 4 + kk
                                for mt in range(4):
                                    lhsT = slab[:, kk * 512 + mt * 128: kk * 512 + mt * 128 + 128]
                                    for th in range(2):
                                        ins = t.matmul(bank(mt * 2 + th), lhsT, xln_rhs(k, w, th),
                                                       start=False, stop=(last and kk == 3))
                            ins.then_inc(sPE, 1)
                            j += 1

            @blk.gpsimd
            def _(g_):
                for j in range(2, 144):
                    gg, rem = divmod(j, 24)
                    w_, k4_ = divmod(rem, 8)
                    if j >= 4:
                        g_.wait_ge(sPE, pe_slab(j - 4))
                    src = bass.AP(convw_d.tensor,
                                  w_ * DM * FOUT + (k4_ * 512) * FOUT + gg * 512,
                                  [[FOUT, 128], [128 * FOUT, 4], [1, 512]])
                    dst = ap3(slab_buf(j).tensor, slab_buf(j).offset, [[512, 4], [1, 512]])
                    g_.dma_start(dst, src).then_inc(qG, 16)
                g_.wait_ge(sDV, dv_halo)
                g_.wait_ge(sPE, pe_slab(143))
                for k in range(8):
                    g_.dma_start(A16[:, k * 1024:(k + 1) * 1024],
                                 wq_d[k * 128:(k + 1) * 128, :]).then_inc(qG, 16)
                for k in range(8):
                    g_.dma_start(A16[:, 8192 + k * 1024: 8192 + (k + 1) * 1024],
                                 wk_d[k * 128:(k + 1) * 128, :]).then_inc(qG, 16)
                for k in range(8):
                    g_.dma_start(A16[:, 16384 + k * 1024: 16384 + (k + 1) * 1024],
                                 wv_d[k * 128:(k + 1) * 128, :]).then_inc(qG, 16)

            @blk.vector
            def _(v):
                for g in range(6):
                    v.wait_ge(sPE, pe_stop(g))
                    for b in range(4, 8):
                        mt, th = divmod(b, 2)
                        v.tensor_copy(QKVR[:, (4 * g + mt) * 1024 + th * 512:
                                            (4 * g + mt) * 1024 + th * 512 + 512],
                                      bank(b)).then_inc(sDV, 1)

            @blk.scalar
            def _(sc):
                for g in range(6):
                    sc.wait_ge(sPE, pe_stop(g))
                    for b in range(0, 4):
                        mt, th = divmod(b, 2)
                        sc.copy(QKVR[:, (4 * g + mt) * 1024 + th * 512:
                                      (4 * g + mt) * 1024 + th * 512 + 512],
                                bank(b)).then_inc(sAC, 1)

        if "t_conv" in tap_out:
            _dump(tap_out["t_conv"], QKVR[:, :])

        if last_phase < 3:
            return nc
        pe3_0 = pe_stop(5)
        pe3 = lambda G: pe3_0 + G + 1
        dv3_0 = dv2_evac(5)
        dv3 = lambda G: dv3_0 + G + 1
        ac3_0 = ac2_evac(5)
        wo_val = lambda k: wqkv_val(23) + 16 * (k + 1)

        with nc.Block("p3") as blk:
            @blk.tensor
            def _(t):
                G = 0
                for kind in range(2):
                    t.wait_ge(qG, wqkv_val(7 + 8 * kind))
                    for m in range(8):
                        for th in range(2):
                            if G >= 4:
                                t.wait_ge(sDV, dv3(G - 4))
                            bk_ = G % 4
                            for k in range(8):
                                ins = t.matmul(
                                    bank(bk_),
                                    A16[:, kind * 8192 + k * 1024 + m * 128:
                                        kind * 8192 + k * 1024 + m * 128 + 128],
                                    QKVR[:, (kind * 8 + k) * 1024 + th * 512:
                                         (kind * 8 + k) * 1024 + th * 512 + 512],
                                    start=(k == 0), stop=(k == 7))
                            ins.then_inc(sPE, 1)
                            G += 1
                t.wait_ge(qG, wqkv_val(23))
                for tt in range(8):
                    for hh in range(2):
                        if G >= 4:
                            t.wait_ge(sDV, dv3(G - 4))
                        bk_ = G % 4
                        for k in range(8):
                            t.matmul(bank(bk_),
                                     QKVR[:, (16 + k) * 1024 + tt * 128:
                                          (16 + k) * 1024 + tt * 128 + 128],
                                     A16[:, 16384 + k * 1024 + hh * 512:
                                         16384 + k * 1024 + hh * 512 + 512],
                                     start=(k == 0), stop=False)
                        ins = t.matmul(bank(bk_), ones_r[0:1, 0:128],
                                       ATTNW[0:1, FOUT + hh * 512: FOUT + hh * 512 + 512],
                                       start=False, stop=True)
                        ins.then_inc(sPE, 1)
                        G += 1

            @blk.gpsimd
            def _(g_):
                g_.wait_ge(sPE, pe3(15))
                for k in range(8):
                    g_.dma_start(A16[:, k * 1024:(k + 1) * 1024],
                                 wo_d[k * 128:(k + 1) * 128, :]).then_inc(qG, 16)

            @blk.vector
            def _(v):
                G = 0
                for kind in range(2):
                    for m in range(8):
                        for th in range(2):
                            v.wait_ge(sPE, pe3(G))
                            dst = XLNR[:, kind * 8192 + m * 1024 + th * 512:
                                       kind * 8192 + m * 1024 + th * 512 + 512]
                            if kind == 0:
                                v.tensor_scalar(dst, bank(G % 4), bq2[:, m:m + 1], 0.125,
                                                ALU.add, ALU.mult).then_inc(sDV, 1)
                            else:
                                v.tensor_scalar(dst, bank(G % 4), bk2[:, m:m + 1], None,
                                                ALU.add).then_inc(sDV, 1)
                            G += 1
                for tt in range(8):
                    for hh in range(2):
                        v.wait_ge(sPE, pe3(G))
                        v.tensor_copy(XLNR[:, 16384 + tt * 1024 + hh * 512:
                                           16384 + tt * 1024 + hh * 512 + 512],
                                      bank(G % 4)).then_inc(sDV, 1)
                        G += 1

        if "t_qkvh" in tap_out:
            _dump(tap_out["t_qkvh"], XLNR[:, 0:24576])

        if last_phase < 4:
            return nc
        pe4_0 = pe3(47)
        dv4_0 = dv3(47)
        ac4_0 = ac3_0
        qt4_0 = qt_tp1h

        order = []
        for b in range(8):
            order.append(("sc", b))
            if b >= 1:
                order.append(("pv", b - 1))
                if b - 1 == 3:
                    order.append(("wo", 0))
        order.append(("pv", 7))
        order.append(("wo", 1))

        pe_sc = {}; pe_pv = {}; pe_wo = {}
        cnt = pe4_0
        for kind, idx in order:
            if kind == "sc":
                for hg in range(4):
                    cnt += 1
                    pe_sc[(idx, hg)] = cnt
            elif kind == "pv":
                for j in range(8):
                    cnt += 1
                    pe_pv[(idx, j)] = cnt
            else:
                for wt in range(8):
                    cnt += 1
                    pe_wo[(idx, wt)] = cnt
        pe4_end = cnt

        dorder = []
        for b in range(8):
            dorder.append(("sm", b))
            if b >= 1:
                dorder.append(("oe", b - 1))
                if b - 1 == 3:
                    dorder.append(("we", 0))
        dorder.append(("oe", 7))
        dorder.append(("we", 1))

        dv_ma = {}; dv_rm = {}; dv_pn = {}; dv_oe = {}; dv_we = {}
        cnt = dv4_0
        for kind, idx in dorder:
            if kind == "sm":
                for hg in range(4):
                    cnt += 1; dv_ma[(idx, hg)] = cnt
                for hg in range(4):
                    cnt += 1; dv_pn[(idx, hg, "ss")] = cnt
                    cnt += 1; dv_pn[(idx, hg, "r")] = cnt
                    cnt += 1; dv_pn[(idx, hg)] = cnt
            elif kind == "oe":
                for j in range(8):
                    cnt += 1
                    dv_oe[(idx, j)] = cnt
            else:
                for wt in range(8):
                    cnt += 1
                    dv_we[(idx, wt)] = cnt
        dv4_end = cnt

        ac_exp = {}
        cnt = ac4_0
        for b in range(8):
            for hg in range(4):
                cnt += 1
                ac_exp[(b, hg)] = cnt
        ac4_end = cnt
        qt_pt = lambda b: qt4_0 + 16 * (b + 1)

        def qh_ap(h, b):
            return XLNR[(h % 2) * 64:(h % 2) * 64 + 64,
                        (h // 2) * 1024 + b * 128:(h // 2) * 1024 + b * 128 + 128]

        def kh_ap(h, b):
            return XLNR[(h % 2) * 64:(h % 2) * 64 + 64,
                        8192 + (h // 2) * 1024 + b * 128: 8192 + (h // 2) * 1024 + b * 128 + 128]

        def vh_ap(h, b):
            return XLNR[:, 16384 + b * 1024 + h * 64: 16384 + b * 1024 + h * 64 + 64]

        def pall(b, h0, wd):
            return QKVR[:, 8192 + (b % 4) * 2048 + h0 * 128:
                        8192 + (b % 4) * 2048 + h0 * 128 + wd]

        def ptall(b, h):
            return QKVR[:, 16384 + (b % 4) * 2048 + h * 128:
                        16384 + (b % 4) * 2048 + h * 128 + 128]

        def ofm_col(ch, b):
            return ch * 1024 + (b // 4) * 512 + (b % 4) * 128

        x0_val = {}

        with nc.Block("p4") as blk:
            @blk.gpsimd
            def _(g_):
                cnt = wo_val(7)
                for t_ in range(4):
                    g_.dma_start(x0b(t_), x_in[1 + 128 * t_: 129 + 128 * t_, :]).then_inc(qG, 16)
                    cnt += 16
                    x0_val[t_] = cnt

            @blk.tensor
            def _(t):
                for kind, idx in order:
                    if kind == "sc":
                        b = idx
                        for hg in range(4):
                            if b > 0 or hg > 0:
                                pb, phg = (b, hg - 1) if hg > 0 else (b - 1, 3)
                                t.wait_ge(sDV, dv_ma[(pb, phg)])
                            for h in range(hg * 4, hg * 4 + 4):
                                ins = t.matmul(bank(h % 4, 128), qh_ap(h, b), kh_ap(h, b),
                                               start=True, stop=True)
                            ins.then_inc(sPE, 1)
                    elif kind == "pv":
                        b = idx
                        t.wait_ge(qT, qt_pt(b))
                        for j in range(8):
                            bA = 4 + 2 * (j % 2)
                            if j >= 2:
                                t.wait_ge(sDV, dv_oe[(b, j - 2)])
                            elif b >= 1:
                                t.wait_ge(sDV, dv_oe[(b - 1, j + 6)])
                            if b == 4 and j < 2:
                                t.wait_ge(sDV, dv_we[(0, 6 + j)])
                            t.matmul(bank(bA, 128, p0=0, p1=64), vh_ap(2 * j, b),
                                     ptall(b, 2 * j), start=True, stop=True)
                            ins = t.matmul(bank(bA + 1, 128, p0=64, p1=128), vh_ap(2 * j + 1, b),
                                           ptall(b, 2 * j + 1), start=True, stop=True)
                            ins.then_inc(sPE, 1)
                    else:
                        g = idx
                        t.wait_ge(qG, wo_val(7))
                        for wt in range(8):
                            bk_ = 4 + (wt % 2)
                            if wt >= 2:
                                t.wait_ge(sDV, dv_we[(g, wt - 2)])
                            else:
                                t.wait_ge(sDV, dv_oe[(3 if g == 0 else 7, 6 + wt)])
                            for k in range(8):
                                ins = t.matmul(bank(bk_),
                                               A16[:, k * 1024 + wt * 128: k * 1024 + wt * 128 + 128],
                                               QKVR[:, k * 1024 + g * 512: k * 1024 + g * 512 + 512],
                                               start=(k == 0), stop=(k == 7))
                            ins.then_inc(sPE, 1)

            @blk.vector
            def _(v):
                for kind, idx in dorder:
                    if kind == "sm":
                        b = idx
                        for hg in range(4):
                            v.wait_ge(sPE, pe_sc[(b, hg)])
                            gi = b * 4 + hg
                            if gi >= 2:
                                pb, phg = divmod(gi - 2, 4)
                                v.wait_ge(sAC, ac_exp[(pb, phg)])
                            sl = (hg % 2) * 512
                            src = bass.AP(PS, 0, [[4096, 128], [512, 4], [1, 128]])
                            dst = bass.AP(s_m4, sl, [[1024, 128], [128, 4], [1, 128]])
                            v.tensor_tensor(dst, src,
                                            bass.AP(mask4, 0, [[512, 128], [128, 4], [1, 128]]),
                                            ALU.add).then_inc(sDV, 1)
                        for hg in range(4):
                            v.wait_ge(sAC, ac_exp[(b, hg)])
                            v.tensor_reduce(
                                bass.AP(sums4, (hg % 2) * 4, [[8, 128], [1, 4], [1, 1]]),
                                bass.AP(QKVR, 8192 + (b % 4) * 2048 + hg * 512,
                                        [[24576, 128], [128, 4], [1, 128]]),
                                AX, ALU.add).then_inc(sDV, 1)
                            v.wait_ge(sDV, dv_pn[(b, hg, "ss")])
                            v.reciprocal(rsum4[:, (hg % 2) * 4:(hg % 2) * 4 + 4],
                                         sums4[:, (hg % 2) * 4:(hg % 2) * 4 + 4]).then_inc(sDV, 1)
                            v.wait_ge(sDV, dv_pn[(b, hg, "r")])
                            for h in range(hg * 4, hg * 4 + 4):
                                ins = v.tensor_scalar(pall(b, h, 128), pall(b, h, 128),
                                                      rsum4[:, (hg % 2) * 4 + h % 4:
                                                            (hg % 2) * 4 + h % 4 + 1],
                                                      None, ALU.mult)
                            ins.then_inc(sDV, 1)
                    elif kind == "oe":
                        b = idx
                        for j in range(8):
                            bA = 4 + 2 * (j % 2)
                            v.wait_ge(sPE, pe_pv[(b, j)])
                            c0 = ofm_col(j, b)
                            v.tensor_copy(QKVR[0:64, c0:c0 + 128],
                                          bank(bA, 128, p0=0, p1=64))
                            v.tensor_copy(QKVR[64:128, c0:c0 + 128],
                                          bank(bA + 1, 128, p0=64, p1=128)).then_inc(sDV, 1)
                    else:
                        g = idx
                        for wt in range(8):
                            v.wait_ge(sPE, pe_wo[(g, wt)])
                            v.tensor_scalar(ATTNW[:, wt * 1024 + g * 512: wt * 1024 + g * 512 + 512],
                                            bank(4 + (wt % 2)), bo2[:, wt:wt + 1], None,
                                            ALU.add).then_inc(sDV, 1)

            @blk.scalar
            def _(sc):
                for b in range(8):
                    for hg in range(4):
                        gi = b * 4 + hg
                        sc.wait_ge(sDV, dv_ma[(b, hg)])
                        if b >= 4 and hg == 0:
                            sc.wait_ge(qT, qt_pt(b - 4))
                        sc.activation(pall(b, hg * 4, 512),
                                      s_m4[:, (hg % 2) * 512:(hg % 2) * 512 + 512],
                                      AF.Exp, bias=neg16[:, :]).then_inc(sAC, 1)
                    sc.wait_ge(sDV, dv_pn[(b, 3)])
                    if b >= 4:
                        sc.wait_ge(sPE, pe_pv[(b - 4, 7)])
                    dst = ap3(QKVR, 16384 + (b % 4) * 2048, [[128, 16], [1, 128]])
                    sc.dma_start(dst, pall(b, 0, 2048), transpose=True).then_inc(qT, 16)

        if "t_attnw" in tap_out:
            _dump(tap_out["t_attnw"], ATTNW[:, :])
        if "t_ofm" in tap_out:
            _dump(tap_out["t_ofm"], QKVR[:, 0:8192])
        if "t_pall" in tap_out:
            _dump(tap_out["t_pall"], QKVR[:, 8192:16384])
        if "t_sm" in tap_out:
            _dump(tap_out["t_sm"], s_m4[:, :])
            _dump(tap_out["t_sm2"], negmax4[:, :])
            _dump(tap_out["t_sm3"], sums4[:, :])
            _dump(tap_out["t_sm5"], rsum4[:, :])

        if last_phase < 5:
            return nc
        pe5_0 = pe4_end
        dv5_0 = dv4_end
        qg5_0 = wo_val(7) + 4 * 16
        projb_val = qg5_0 + 3 * 16
        projw_val = lambda gs: x0_val[("pw", gs)]
        pe_p5 = lambda gs: pe5_0 + gs + 1
        dv_p5 = lambda gs: dv5_0 + gs + 1 + 4 * (gs // 8)
        dv5_add = lambda t_: dv5_0 + 12 * (t_ // 4) + 9 + (t_ % 4)

        with nc.Block("p5") as blk:
            @blk.gpsimd
            def _(g_):
                cnt = [qg5_0]
                for t_ in (4, 5):
                    g_.dma_start(x0b(t_), x_in[1 + 128 * t_: 129 + 128 * t_, :]).then_inc(qG, 16)
                    cnt[0] += 16
                    x0_val[t_] = cnt[0]
                g_.dma_start(QKVR[0:1, 0:DM], projb_d[None, :]).then_inc(qG, 16)
                cnt[0] += 16
                projb_val2 = cnt[0]
                for g in range(2):
                    for s in range(8):
                        gs = g * 8 + s
                        if gs >= 3:
                            g_.wait_ge(sPE, pe_p5(gs - 3))
                        src = bass.AP(projw_d.tensor, s * 512,
                                      [[DM, 128], [128 * DM, 8], [1, 512]])
                        dst = ap3(QKVR, 4096 + (gs % 3) * 4096, [[512, 8], [1, 512]])
                        g_.dma_start(dst, src).then_inc(qG, 16)
                        cnt[0] += 16
                        x0_val[("pw", gs)] = cnt[0]
                        if gs == 9:
                            g_.wait_ge(sDV, dv5_add(0))
                            g_.dma_start(x0b(6), x_in[1 + 128 * 6: 129 + 128 * 6, :]).then_inc(qG, 16)
                            cnt[0] += 16
                            x0_val[6] = cnt[0]
                        if gs == 10:
                            g_.wait_ge(sDV, dv5_add(1))
                            g_.dma_start(x0b(7), x_in[1 + 128 * 7: 129 + 128 * 7, :]).then_inc(qG, 16)
                            cnt[0] += 16
                            x0_val[7] = cnt[0]

            @blk.tensor
            def _(t):
                t.wait_ge(qG, projb_val)
                for g in range(2):
                    for s in range(8):
                        gs = g * 8 + s
                        if gs >= 2:
                            t.wait_ge(sDV, dv_p5(gs - 2))
                        t.wait_ge(qG, projw_val(gs))
                        for k in range(8):
                            rhs = QKVR[:, 4096 + (gs % 3) * 4096 + k * 512:
                                       4096 + (gs % 3) * 4096 + k * 512 + 512]
                            for tb in range(4):
                                t.matmul(bank(tb + 4 * (s % 2)),
                                         ATTNW[:, k * 1024 + g * 512 + tb * 128:
                                               k * 1024 + g * 512 + tb * 128 + 128],
                                         rhs, start=(k == 0), stop=False)
                        for tb in range(4):
                            ins = t.matmul(bank(tb + 4 * (s % 2)), ones_r[0:1, 0:128],
                                           QKVR[0:1, s * 512: s * 512 + 512],
                                           start=False, stop=True)
                        ins.then_inc(sPE, 1)

            @blk.vector
            def _(v):
                for g in range(2):
                    for s in range(8):
                        gs = g * 8 + s
                        v.wait_ge(sPE, pe_p5(gs))
                        for tb in range(4):
                            ins = v.tensor_copy(x1v(g * 4 + tb, s * 512, 512),
                                                bank(tb + 4 * (s % 2)))
                        ins.then_inc(sDV, 1)
                        if s == 7:
                            v.wait_ge(sDV, dv_p5(g * 8 + 7))
                            for tb in range(4):
                                t_ = g * 4 + tb
                                v.wait_ge(qG, x0_val[t_])
                                v.tensor_tensor(x1v(t_, 0, 4096), x1v(t_, 0, 4096),
                                                x0b(t_), ALU.add).then_inc(sDV, 1)

        if "t_x1a" in tap_out:
            _dump(tap_out["t_x1a"], XLNR[:, 0:32768])

        if last_phase < 6:
            return nc
        dv6_0 = dv5_add(7)
        ac6_0 = ac4_end
        qs6_0 = qs_p1_end
        qt6_0 = qt_pt(7)
        dv6_sum = lambda t_: dv6_0 + 5 * t_ + 1
        dv6_nm = lambda t_: dv6_0 + 5 * t_ + 2
        dv6_t1 = lambda t_: dv6_0 + 5 * t_ + 3
        dv6_pre = lambda t_: dv6_0 + 5 * t_ + 4
        dv6_ln = lambda t_: dv6_0 + 5 * t_ + 5
        ac6_sq = lambda t_: ac6_0 + 3 * t_ + 1
        ac6_ln = lambda t_: ac6_0 + 3 * t_ + 2
        ac6_rstd = lambda t_: ac6_0 + 3 * t_ + 3
        qs6_0b = qs_p1_end
        qs6_st = lambda t_: qs6_0b + 16 * (t_ + 1)
        qt6_tp = lambda t_: qt6_0 + 16 * (t_ + 1)

        def xln2tm(t_):
            return QKVR[:, (t_ % 2) * 4096:(t_ % 2) * 4096 + 4096]

        with nc.Block("p6") as blk:
            @blk.sync
            def _(s):
                for t_ in range(n6):
                    s.wait_ge(qT, qt6_tp(t_))
                    dst = bass.AP(xln2_d.tensor, 128 * t_,
                                  [[T, 128], [128 * T, 32], [1, 128]])
                    s.dma_start(dst, ap3(QKVR, 8192 + (t_ % 2) * 4096, [[128, 32], [1, 128]])
                                ).then_inc(qS, 16)

            @blk.vector
            def _(v):
                for t_ in range(n6):
                    v.reduce_sum(st(t_, 0), x1v(t_, 0, 4096), axis=AX).then_inc(sDV, 1)
                    v.wait_ge(sDV, dv6_sum(t_))
                    v.tensor_scalar(st(t_, 2), st(t_, 0), -1.0 / DM, None, ALU.mult).then_inc(sDV, 1)
                    v.wait_ge(sAC, ac6_sq(t_))
                    v.wait_ge(sDV, dv6_nm(t_))
                    v.tensor_tensor(st(t_, 3), st(t_, 2), st(t_, 2), ALU.mult)
                    v.tensor_scalar(st(t_, 4), st(t_, 1), 1.0 / DM, None,
                                    ALU.mult).then_inc(sDV, 1)
                    v.wait_ge(sDV, dv6_t1(t_))
                    v.tensor_tensor(st(t_, 6), st(t_, 4), st(t_, 3),
                                    ALU.subtract).then_inc(sDV, 1)
                    v.wait_ge(sAC, ac6_rstd(t_))
                    v.tensor_scalar(xln2tm(t_), x1v(t_, 0, 4096), st(t_, 2), st(t_, 5),
                                    ALU.add, ALU.mult).then_inc(sDV, 1)

            @blk.scalar
            def _(sc):
                for t_ in range(n6):
                    if t_ >= 2:
                        sc.wait_ge(qT, qt6_tp(t_ - 2))
                    sc.activation(xln2tm(t_), x1v(t_, 0, 4096), AF.Square,
                                  accum_out=st(t_, 1)).then_inc(sAC, 1)
                    sc.wait_ge(sDV, dv6_pre(t_))
                    sc.activation(st(t_, 5), st(t_, 6), AF.Ln, bias=epsc[:, :]).then_inc(sAC, 1)
                    sc.wait_ge(sAC, ac6_ln(t_))
                    sc.activation(st(t_, 5), st(t_, 5), AF.Exp, scale=-0.5).then_inc(sAC, 1)
                    sc.wait_ge(sDV, dv6_ln(t_))
                    if t_ >= 2:
                        sc.wait_ge(qS, qs6_st(t_ - 2))
                    if "notp" in p6dbg:
                        sc.dma_start(QKVR[:, 8192 + (t_ % 2) * 4096:
                                          8192 + (t_ % 2) * 4096 + 4096],
                                     xln2tm(t_)).then_inc(qT, 16)
                    else:
                        dst = ap3(QKVR, 8192 + (t_ % 2) * 4096, [[128, 32], [1, 128]])
                        sc.dma_start(dst, xln2tm(t_), transpose=True).then_inc(qT, 16)

            @blk.gpsimd
            def _(g_):
                g_.dma_start(ap3(SLAB, 0, [[1024, 2], [1, 1024]]),
                             bass.AP(up_d.tensor, 0, [[MID, 128], [128 * MID, 2], [1, 1024]])
                             ).then_inc(qG, 16)
                g_.dma_start(ap3(SLAB, 2048, [[256, 8], [1, 256]]),
                             bass.AP(down_d.tensor, 0, [[256, 128], [128 * 256, 8], [1, 256]])
                             ).then_inc(qG, 16)

        if "t_x1" in tap_out:
            _dump(tap_out["t_x1"], XLNR[:, 0:32768])
        if "t_xln2" in tap_out:
            _dump_dram(tap_out["t_xln2"], xln2_d)

        if last_phase < 7:
            return nc
        qg7_0 = projw_val(15) + 2 * 16
        pe7_0 = pe_p5(15)
        dv7_0 = dv6_ln(7)
        ac7_0 = ac6_rstd(7)
        qs7_end = qs6_st(n6 - 1)

        qg7_rec = {}

        def qg7_gat(h):
            return qg7_rec[("gat", h)]

        def qg7_w(h):
            return qg7_rec[("w", h)]

        def qg7_scat(h, dt):
            return qg7_rec[("scat", h, dt)]

        pe_up7 = lambda h, u2: pe7_0 + 8 * h + u2 + 1
        pe_dn7 = lambda h, i: pe7_0 + 8 * h + 4 + i + 1
        ac_g7 = lambda h, u2, th: ac7_0 + 8 * h + u2 * 2 + th + 1
        dv_e7 = lambda h, i: dv7_0 + 4 * h + i + 1

        def xp(h, c):
            return QKVR[:, 16384 + c * 2048 + (h % 2) * 1024:
                        16384 + c * 2048 + (h % 2) * 1024 + 1024]

        def upb(h, c):
            base = SLAB if h % 2 == 0 else QKVR
            return base[:, c * 1024: c * 1024 + 1024]

        def dnb(h):
            base = SLAB if h % 2 == 0 else QKVR
            return base[:, 2048:4096]

        def midv(th, c0, wd):
            return ATTNW[:, th * 4096 + c0: th * 4096 + c0 + wd]

        def dout_full(h, dt):
            return QKVR[:, 20480 + dt * 2048 + (h % 2) * 1024:
                        20480 + dt * 2048 + (h % 2) * 1024 + 1024]

        def dout(h, dt, th):
            c = 20480 + dt * 2048 + (h % 2) * 1024 + th * 512
            return QKVR[:, c: c + 512]

        with nc.Block("p7") as blk:
            @blk.gpsimd
            def _(g_):
                cnt = [qg7_0]

                def dma(instr):
                    cnt[0] += 16
                    instr.then_inc(qG, 16)

                def emit_scat(h):
                    for dt in range(2):
                        g_.wait_ge(sDV, dv_e7(h, dt * 2 + 1))
                        dma(g_.indirect_dma_start(
                            scat_d, IndirectOffsetOnAxis(perm2[:, 2 * h + dt: 2 * h + dt + 1], 0),
                            dout_full(h, dt), None))
                        qg7_rec[("scat", h, dt)] = cnt[0]

                g_.wait_ge(qS, qs7_end)
                g_.wait_ge(qT, qt6_tp(7))
                qg7_rec[("w", 0)] = qg7_0
                for h in range(16):
                    if h >= 2:
                        g_.wait_ge(sPE, pe_up7(h - 2, 3))
                    for c in range(2):
                        dma(g_.indirect_dma_start(
                            xp(h, c), None, xln2_d,
                            IndirectOffsetOnAxis(perm2[:, 2 * h + c: 2 * h + c + 1], 0)))
                    qg7_rec[("gat", h)] = cnt[0]
                    if h >= 1:
                        if h >= 2:
                            g_.wait_ge(sPE, pe_dn7(h - 2, 3))
                        base = SLAB if h % 2 == 0 else QKVR
                        dma(g_.dma_start(ap3(base, 0, [[1024, 2], [1, 1024]]),
                                         bass.AP(up_d.tensor, h * 256 * MID,
                                                 [[MID, 128], [128 * MID, 2], [1, 1024]])))
                        dma(g_.dma_start(ap3(base, 2048, [[256, 8], [1, 256]]),
                                         bass.AP(down_d.tensor, h * MID * 256,
                                                 [[256, 128], [128 * 256, 8], [1, 256]])))
                        qg7_rec[("w", h)] = cnt[0]
                    if h >= 2:
                        emit_scat(h - 2)
                emit_scat(14)
                emit_scat(15)

            @blk.tensor
            def _(t):
                for h in range(16):
                    t.wait_ge(qG, qg7_gat(h))
                    t.wait_ge(qG, qg7_w(h))
                    for u2 in range(4):
                        B = (u2 % 2) * 4
                        if u2 >= 2:
                            t.wait_ge(sAC, ac_g7(h, u2 - 2, 1))
                        elif h >= 1:
                            if u2 == 0:
                                t.wait_ge(sAC, ac_g7(h - 1, 2, 1))
                            else:
                                t.wait_ge(sDV, dv_e7(h - 1, 3))
                        for mq in range(2):
                            m = u2 * 2 + mq
                            for kd in range(2):
                                lhsT = upb(h, kd)[:, m * 128: m * 128 + 128]
                                for th in range(2):
                                    ins = t.matmul(bank(B + mq * 2 + th),
                                                   lhsT, xp(h, kd)[:, th * 512: th * 512 + 512],
                                                   start=(kd == 0), stop=(kd == 1))
                        ins.then_inc(sPE, 1)
                    t.wait_ge(sAC, ac_g7(h, 3, 1))
                    for kd in range(8):
                        for dt in range(2):
                            lhsT = dnb(h)[:, kd * 256 + dt * 128: kd * 256 + dt * 128 + 128]
                            for th in range(2):
                                ins = t.matmul(bank(4 + dt * 2 + th), lhsT,
                                               midv(th, kd * 512, 512),
                                               start=(kd == 0), stop=(kd == 7))
                                if kd == 7:
                                    ins.then_inc(sPE, 1)

            @blk.scalar
            def _(sc):
                for h in range(16):
                    for u2 in range(4):
                        B = (u2 % 2) * 4
                        sc.wait_ge(sPE, pe_up7(h, u2))
                        if h >= 1:
                            sc.wait_ge(sPE, pe_dn7(h - 1, 3))
                        src = bass.AP(PS, B * 512,
                                      [[4096, 128], [512, 2], [1024, 2], [1, 512]])
                        dst = bass.AP(ATTNW, u2 * 1024,
                                      [[8192, 128], [4096, 2], [512, 2], [1, 512]])
                        sc.activation(dst, src, AF.Gelu).then_inc(sAC, 2)

            @blk.vector
            def _(v):
                for h in range(16):
                    for dt in range(2):
                        for th in range(2):
                            v.wait_ge(sPE, pe_dn7(h, dt * 2 + th))
                            if h >= 2:
                                v.wait_ge(qG, qg7_rec[("scat", h - 2, dt)])
                            v.tensor_copy(dout(h, dt, th),
                                          bank(4 + dt * 2 + th)).then_inc(sDV, 1)

        if "t_scat" in tap_out:
            _dump_dram(tap_out["t_scat"], scat_d)

        if last_phase < 8:
            return nc
        qg8_end = qg7_scat(15, 1)
        qt8_0 = qt6_tp(7)
        dv8_0 = dv_e7(15, 3)
        qs8_0 = qs7_end
        qt8_tp = lambda t_: qt8_0 + 16 * (t_ + 1)
        dv8_add = lambda t_: dv8_0 + t_ + 1
        qs8_st = lambda t_: qs8_0 + 16 * (t_ + 1)

        def scat_tm(t_):
            return A16[:, 16384 + (t_ % 2) * DM: 16384 + (t_ % 2) * DM + DM]

        def out_st(t_):
            return A32[:, (t_ % 2) * DM:(t_ % 2) * DM + DM]

        with nc.Block("p8") as blk:
            @blk.scalar
            def _(sc):
                sc.wait_ge(qG, qg8_end)
                for t_ in range(NT):
                    if t_ >= 2:
                        sc.wait_ge(sDV, dv8_add(t_ - 2))
                    sc.dma_start(scat_tm(t_), scat_d[:, 128 * t_: 128 * (t_ + 1)],
                                 transpose=True).then_inc(qT, 16)

            @blk.sync
            def _(s):
                for t_ in range(0, NT, 2):
                    s.wait_ge(sDV, dv8_add(t_))
                    s.dma_start(out_d[128 * t_:128 * (t_ + 1), :], out_st(t_)).then_inc(qS, 16)
                s.wait_ge(qS, qs8_0 + 16 * 4)
                s.wait_ge(qG, qg8_end + 16 * 4)

            @blk.vector
            def _(v):
                for t_ in range(NT):
                    v.wait_ge(qT, qt8_0 + 16 * (t_ + 1))
                    if t_ >= 2:
                        tp_ = t_ - 2
                        if tp_ % 2 == 0:
                            v.wait_ge(qS, qs8_0 + 16 * (tp_ // 2 + 1))
                        else:
                            v.wait_ge(qG, qg8_end + 16 * (tp_ // 2 + 1))
                    v.tensor_tensor(out_st(t_), scat_tm(t_), x1v(t_, 0, 4096),
                                    ALU.add).then_inc(sDV, 1)

            @blk.gpsimd
            def _(g_):
                for t_ in range(1, NT, 2):
                    g_.wait_ge(sDV, dv8_add(t_))
                    g_.dma_start(out_d[128 * t_:128 * (t_ + 1), :], out_st(t_)).then_inc(qG, 16)

    return nc


def make_inputs(full):
    x = np.asarray(full["L0_attn_inputs"], dtype=np.float32)
    conv_w = np.ascontiguousarray(np.asarray(full["L0_attn_conv_w"], dtype=np.float32))
    conv_b = np.ascontiguousarray(np.asarray(full["L0_attn_conv_b"], dtype=np.float32))
    wq = np.ascontiguousarray(np.asarray(full["L0_attn_wq"], dtype=np.float32).reshape(W, W))
    wk = np.ascontiguousarray(np.asarray(full["L0_attn_wk"], dtype=np.float32).reshape(W, W))
    wv = np.ascontiguousarray(np.asarray(full["L0_attn_wv"], dtype=np.float32).reshape(W, W))
    bq = np.ascontiguousarray(np.asarray(full["L0_attn_bq"], dtype=np.float32).reshape(8, 128).T)
    bk = np.ascontiguousarray(np.asarray(full["L0_attn_bk"], dtype=np.float32).reshape(8, 128).T)
    bv = np.ascontiguousarray(np.asarray(full["L0_attn_bv"], dtype=np.float32).reshape(W))
    wo = np.ascontiguousarray(np.asarray(full["L0_attn_wo"], dtype=np.float32).reshape(W, W))
    bo = np.ascontiguousarray(np.asarray(full["L0_attn_bo"], dtype=np.float32).reshape(8, 128).T)
    projw = np.ascontiguousarray(np.asarray(full["L0_attn_proj_w"], dtype=np.float32))
    projb = np.ascontiguousarray(np.asarray(full["L0_attn_proj_b"], dtype=np.float32))
    up = np.ascontiguousarray(np.asarray(full["L0_ffn_up"], dtype=np.float32))
    down = np.ascontiguousarray(np.asarray(full["L0_ffn_down"], dtype=np.float32))
    perm = np.ascontiguousarray(
        np.asarray(full["L0_ffn_perm"]).reshape(32, 128).T).astype(np.uint32)

    tril = np.tril(np.ones((128, 128), np.float32))
    mask = np.where(tril > 0, 0.0, np.float32(-30000.0)).astype(np.float32)
    mask4 = np.ascontiguousarray(np.tile(mask, (1, 4)))

    shared = dict(convw=conv_w, convb=conv_b, wq=wq, wk=wk, wv=wv,
                  bq2d=bq, bk2d=bk, bv=bv, wo=wo, bo2d=bo,
                  projw=projw, projb=projb, ffup=up, ffdown=down,
                  perm2d=perm, mask4d=mask4)

    in_maps = []
    for core in range(8):
        b, q0 = divmod(core * T, 4096)
        lo = max(q0 - 1, 0)
        hi = min(q0 + T + 1, 4096)
        sl = np.asarray(x[b, lo:hi, :], dtype=np.float32)
        pad_lo = 1 - (q0 - lo)
        pad_hi = 1 - (hi - q0 - T)
        if pad_lo or pad_hi:
            sl = np.pad(sl, ((pad_lo, pad_hi), (0, 0)))
        m = dict(shared)
        m["x_in"] = np.ascontiguousarray(sl)
        in_maps.append(m)
    return in_maps


_NC_CACHE = {}


def kernel(**inputs):
    in_maps = make_inputs(inputs)
    if "nc" not in _NC_CACHE:
        _NC_CACHE["nc"] = build_nc()
    nc = _NC_CACHE["nc"]
    res = run_bass_kernel_spmd(nc, in_maps, core_ids=list(range(8)))
    outs = [np.asarray(res.results[c]["out"]) for c in range(8)]
    full = np.concatenate(outs, axis=0).reshape(2, 4096, DM)
    return full.astype(np.float32)


# version 20
# speedup vs baseline: 1.0246x; 1.0246x over previous
import numpy as np
import ml_dtypes
from contextlib import ExitStack

import concourse.bass as bass
import concourse.mybir as mybir
from concourse.bass_utils import run_bass_kernel_spmd
from concourse.bass_core_rust import IndirectOffsetOnAxis

F32 = mybir.dt.float32
BF16 = mybir.dt.bfloat16
U32 = mybir.dt.uint32
AF = mybir.ActivationFunctionType
ALU = mybir.AluOpType
AX = mybir.AxisListType.X

T = 1024
NT = 8
DM = 4096
FOUT = 3072
W = 1024
H = 16
MID = 1024
XP = 1056
XLNR_COLS = 32 * XP


def ap3(t, off, dims):
    free = int(np.prod(t.shape[1:]))
    return bass.AP(t, off, [[free, 128]] + dims)


def build_nc(taps=(), last_phase=8, p6dbg='', n6=8):
    taps = set(taps)
    nc = bass.Bass("TRN2", target_bir_lowering=False)

    x_in = nc.dram_tensor("x_in", [T + 2, DM], F32, kind="ExternalInput").ap()
    convw_d = nc.dram_tensor("convw", [3, DM, FOUT], F32, kind="ExternalInput").ap()
    convb_d = nc.dram_tensor("convb", [FOUT], F32, kind="ExternalInput").ap()
    wq_d = nc.dram_tensor("wq", [W, W], F32, kind="ExternalInput").ap()
    wk_d = nc.dram_tensor("wk", [W, W], F32, kind="ExternalInput").ap()
    wv_d = nc.dram_tensor("wv", [W, W], F32, kind="ExternalInput").ap()
    bq2_d = nc.dram_tensor("bq2d", [128, 8], F32, kind="ExternalInput").ap()
    bk2_d = nc.dram_tensor("bk2d", [128, 8], F32, kind="ExternalInput").ap()
    bv_d = nc.dram_tensor("bv", [W], F32, kind="ExternalInput").ap()
    wo_d = nc.dram_tensor("wo", [W, W], F32, kind="ExternalInput").ap()
    bo2_d = nc.dram_tensor("bo2d", [128, 8], F32, kind="ExternalInput").ap()
    projw_d = nc.dram_tensor("projw", [W, DM], F32, kind="ExternalInput").ap()
    projb_d = nc.dram_tensor("projb", [DM], F32, kind="ExternalInput").ap()
    up_d = nc.dram_tensor("ffup", [H, 256, MID], F32, kind="ExternalInput").ap()
    down_d = nc.dram_tensor("ffdown", [H, MID, 256], F32, kind="ExternalInput").ap()
    perm_d = nc.dram_tensor("perm2d", [128, 32], U32, kind="ExternalInput").ap()
    mask_d = nc.dram_tensor("mask4d", [128, 512], F32, kind="ExternalInput").ap()
    out_d = nc.dram_tensor("out", [T, DM], F32, kind="ExternalOutput").ap()

    xln2_d = nc.dram_tensor("xln2fm", [DM, T], BF16, kind="Internal").ap()
    scat_d = nc.dram_tensor("scat", [DM, T], BF16, kind="Internal").ap()

    tap_specs = {
        "t_stage": [128, 24576],
        "t_xln1": [128, XLNR_COLS],
        "t_conv": [128, 24576],
        "t_qkvh": [128, 24576],
        "t_attnw": [128, 8192],
        "t_ofm": [128, 8192],
        "t_pall": [128, 8192],
        "t_x1a": [128, 32768],
        "t_x1": [128, 32768],
    }
    tap_out = {n: nc.dram_tensor(n, s, BF16, kind="ExternalOutput").ap()
               for n, s in tap_specs.items() if n in taps}
    if "t_stt" in taps:
        tap_out["t_stt"] = nc.dram_tensor("t_stt", [128, 16], F32, kind="ExternalOutput").ap()
    if "t_sm" in taps:
        tap_out["t_sm"] = nc.dram_tensor("t_sm", [128, 1024], F32, kind="ExternalOutput").ap()
        tap_out["t_sm2"] = nc.dram_tensor("t_sm2", [128, 8], F32, kind="ExternalOutput").ap()
        tap_out["t_sm3"] = nc.dram_tensor("t_sm3", [128, 8], F32, kind="ExternalOutput").ap()
        tap_out["t_sm5"] = nc.dram_tensor("t_sm5", [128, 8], F32, kind="ExternalOutput").ap()
    if "t_x0" in taps:
        tap_out["t_x0"] = nc.dram_tensor("t_x0", [128, 8192], F32, kind="ExternalOutput").ap()
    for n, src in (("t_xln2", xln2_d), ("t_scat", scat_d)):
        if n in taps:
            tap_out[n] = nc.dram_tensor(n, [DM, T], BF16, kind="ExternalOutput").ap()

    with ExitStack() as ctx:
        sb = lambda n, s, d: ctx.enter_context(nc.sbuf_tensor(n, s, d))
        XLNR = sb("XLNR", [128, XLNR_COLS], BF16)
        QKVR = sb("QKVR", [128, 24576], BF16)
        ATTNW = sb("ATTNW", [128, 8192], BF16)
        ARENA = sb("ARENA", [128, 49152], mybir.dt.uint8)
        A16 = ARENA.bitcast(BF16)
        A32 = ARENA.bitcast(F32)
        SLAB = sb("SLAB", [128, 4096], BF16)
        ones_r = sb("ones_r", [1, 512], BF16)
        convb_r = ATTNW[0:1, 0:FOUT]
        bv_r = ATTNW[0:1, FOUT:FOUT + W]
        projb_r = QKVR[0:1, 0:DM]
        bq2 = sb("bq2", [128, 8], F32)
        bk2 = sb("bk2", [128, 8], F32)
        bo2 = sb("bo2", [128, 8], F32)
        mask4 = sb("mask4", [128, 512], F32)
        perm2 = sb("perm2", [128, 32], U32)
        epsc = sb("epsc", [128, 1], F32)
        neg16 = sb("neg16", [128, 1], F32)
        halo_fm = sb("halo_fm", [128, 512], BF16)
        s_m4 = sb("s_m4", [128, 1024], F32)
        negmax4 = sb("negmax4", [128, 8], F32)
        sums4 = sb("sums4", [128, 8], F32)
        rsum4 = sb("rsum4", [128, 8], F32)
        stt = sb("stt", [128, 16], F32)
        PS = ctx.enter_context(nc.psum_tensor("PS", [128, 4096], F32))

        qS = ctx.enter_context(nc.semaphore("qS"))
        qG = ctx.enter_context(nc.semaphore("qG"))
        qT = ctx.enter_context(nc.semaphore("qT"))
        qD = ctx.enter_context(nc.semaphore("qD"))
        sPE = ctx.enter_context(nc.semaphore("sPE"))
        sPL = ctx.enter_context(nc.semaphore("sPL"))
        sDV = ctx.enter_context(nc.semaphore("sDV"))
        sAC = ctx.enter_context(nc.semaphore("sAC"))

        dump_cnt = [0]

        def _dump(dst, src_ap):
            with nc.Block() as dblk:
                @dblk.sync
                def _(s):
                    s.dma_start(dst, src_ap).then_inc(qD, 16)
                    dump_cnt[0] += 16
                    s.wait_ge(qD, dump_cnt[0])

        def _dump_dram(dst, src_dram):
            with nc.Block() as dblk:
                @dblk.sync
                def _(s):
                    for c in range(32):
                        s.dma_start(SLAB[:, 0:1024], src_dram[c * 128:(c + 1) * 128, :]).then_inc(qD, 16)
                        dump_cnt[0] += 16
                        s.wait_ge(qD, dump_cnt[0])
                        s.dma_start(dst[c * 128:(c + 1) * 128, :], SLAB[:, 0:1024]).then_inc(qD, 16)
                        dump_cnt[0] += 16
                        s.wait_ge(qD, dump_cnt[0])

        def bank(b, wd=512, off=0, p0=0, p1=128):
            return PS[p0:p1, b * 512 + off: b * 512 + off + wd]

        def slab_buf(j):
            sl = j % 4
            if sl < 2:
                return SLAB[:, sl * 2048: sl * 2048 + 2048]
            return A16[:, (sl - 2) * 2048:(sl - 2) * 2048 + 2048]

        def xln_rhs(k, w, th):
            c = k * XP + 15 + w + th * 512
            return XLNR[:, c: c + 512]

        def x1v(t_, c0, wd):
            return XLNR[:, t_ * 4096 + c0: t_ * 4096 + c0 + wd]

        def x0st(i):
            return A32[:, (i % 2) * DM:(i % 2) * DM + DM]

        def x0b(t_):
            if t_ < 4:
                off = 8192 + t_ * 4096
            elif t_ < 6:
                off = (t_ - 4) * 4096
            else:
                off = 8192 + (t_ - 6) * 4096
            return A16[:, off: off + 4096]

        def xt_st(i):
            return A16[:, 16384 + (i % 2) * DM: 16384 + (i % 2) * DM + DM]

        def st(i, c):
            return stt[:, (i % 2) * 8 + c:(i % 2) * 8 + c + 1]

        def sth(c):
            return stt[0:2, 8 + c: 8 + c + 1]

        qs_aux = 5 * 16
        qs_x0e = lambda i: qs_aux + 16 * (i // 2 + 1)
        qs_p1_end = qs_aux + 16 * 6
        qg_x0 = {1: 3 * 16, 3: 4 * 16, 5: 7 * 16, 7: 8 * 16}
        slab_val = lambda j: 16 * (5 + j) if j < 2 else 16 * (8 + j - 1)
        dv_sum = lambda i: 5 * i + 1
        dv_nm = lambda i: 5 * i + 2
        dv_t1 = lambda i: 5 * i + 3
        dv_pre = lambda i: 5 * i + 4
        dv_apply = lambda i: 5 * i + 5
        dv_sum_h, dv_nm_h, dv_t1_h, dv_pre_h, dv_apply_h, dv_halo = 41, 42, 43, 44, 45, 46
        ac_sq = lambda i: 3 * i + 1
        ac_ln = lambda i: 3 * i + 2
        ac_rstd = lambda i: 3 * i + 3
        ac_sq_h, ac_ln_h, ac_rstd_h = 25, 26, 27
        qt_tp1 = lambda i: 16 * (i + 1)
        qt_tp1h = 16 * 9

        with nc.Block("p1", no_gpsimd_drain=True) as blk:
            @blk.sync
            def _(s):
                s.dma_start(bq2[:, :], bq2_d).then_inc(qS, 16)
                s.dma_start(bk2[:, :], bk2_d).then_inc(qS, 16)
                s.dma_start(bo2[:, :], bo2_d).then_inc(qS, 16)
                s.dma_start(mask4[:, :], mask_d).then_inc(qS, 16)
                s.dma_start(perm2[:, :], perm_d).then_inc(qS, 16)
                for i in range(0, NT, 2):
                    if i >= 2:
                        s.wait_ge(sDV, dv_apply(i - 2))
                    s.dma_start(x0st(i), x_in[1 + 128 * i: 129 + 128 * i, :]).then_inc(qS, 16)
                s.wait_ge(sDV, dv_apply(6))
                s.dma_start(A32[0:1, 0:DM], x_in[0:1, :]).then_inc(qS, 16)
                s.dma_start(A32[1:2, 0:DM], x_in[T + 1:T + 2, :]).then_inc(qS, 16)

            @blk.gpsimd
            def _(g):
                g.dma_start(convb_r, convb_d[None, :]).then_inc(qG, 16)
                g.dma_start(bv_r, bv_d[None, :]).then_inc(qG, 16)
                g.dma_start(x0st(1), x_in[129:257, :]).then_inc(qG, 16)
                g.wait_ge(sDV, dv_apply(1))
                g.dma_start(x0st(3), x_in[1 + 384: 129 + 384, :]).then_inc(qG, 16)
                for j in range(2):
                    src = bass.AP(convw_d.tensor, (j * 512) * FOUT,
                                  [[FOUT, 128], [128 * FOUT, 4], [1, 512]])
                    dst = ap3(slab_buf(j).tensor, slab_buf(j).offset, [[512, 4], [1, 512]])
                    g.dma_start(dst, src).then_inc(qG, 16)
                g.wait_ge(sDV, dv_apply(3))
                g.dma_start(x0st(5), x_in[1 + 640: 129 + 640, :]).then_inc(qG, 16)
                g.wait_ge(sDV, dv_apply(5))
                g.dma_start(x0st(7), x_in[1 + 896: 129 + 896, :]).then_inc(qG, 16)



            @blk.vector
            def _(v):
                v.memset(ones_r[:, :], 1.0)
                v.memset(epsc[:, :], 1e-5)
                v.memset(neg16[:, :], -16.0)
                for i in range(NT):
                    if i % 2 == 0:
                        v.wait_ge(qS, qs_x0e(i))
                    else:
                        v.wait_ge(qG, qg_x0[i])
                    v.reduce_sum(st(i, 0), x0st(i), axis=AX).then_inc(sDV, 1)
                    v.wait_ge(sDV, dv_sum(i))
                    v.tensor_scalar(st(i, 2), st(i, 0), -1.0 / DM, None, ALU.mult).then_inc(sDV, 1)
                    v.wait_ge(sAC, ac_sq(i))
                    v.wait_ge(sDV, dv_nm(i))
                    v.tensor_tensor(st(i, 3), st(i, 2), st(i, 2), ALU.mult)
                    v.tensor_scalar(st(i, 4), st(i, 1), 1.0 / DM, None, ALU.mult).then_inc(sDV, 1)
                    v.wait_ge(sDV, dv_t1(i))
                    v.tensor_tensor(st(i, 6), st(i, 4), st(i, 3), ALU.subtract).then_inc(sDV, 1)
                    v.wait_ge(sAC, ac_rstd(i))
                    if i >= 2:
                        v.wait_ge(qT, qt_tp1(i - 2))
                    v.tensor_scalar(xt_st(i), x0st(i), st(i, 2), st(i, 5),
                                    ALU.add, ALU.mult).then_inc(sDV, 1)
                x0h = A32[0:2, 0:DM]
                xth = A16[0:16, 16384:16384 + DM]
                v.wait_ge(qS, qs_p1_end)
                v.reduce_sum(sth(0), A32[0:2, 0:DM], axis=AX).then_inc(sDV, 1)
                v.wait_ge(sDV, dv_sum_h)
                v.tensor_scalar(sth(2), sth(0), -1.0 / DM, None, ALU.mult).then_inc(sDV, 1)
                v.wait_ge(sAC, ac_sq_h)
                v.wait_ge(sDV, dv_nm_h)
                v.tensor_tensor(sth(3), sth(2), sth(2), ALU.mult)
                v.tensor_scalar(sth(4), sth(1), 1.0 / DM, None, ALU.mult).then_inc(sDV, 1)
                v.wait_ge(sDV, dv_t1_h)
                v.tensor_tensor(sth(6), sth(4), sth(3), ALU.subtract).then_inc(sDV, 1)
                v.wait_ge(sAC, ac_rstd_h)
                v.wait_ge(qT, qt_tp1(6))
                v.tensor_scalar(xth[0:2, :], x0h, sth(2), sth(5),
                                ALU.add, ALU.mult).then_inc(sDV, 1)
                v.wait_ge(qT, qt_tp1h)
                v.tensor_copy(bass.AP(XLNR, 15, [[XLNR_COLS, 128], [XP, 32], [1, 1]]),
                              bass.AP(A16, 20480, [[24576, 128], [128, 32], [1, 1]]))
                v.tensor_copy(bass.AP(XLNR, 1040, [[XLNR_COLS, 128], [XP, 32], [1, 1]]),
                              bass.AP(A16, 20481, [[24576, 128], [128, 32], [1, 1]])).then_inc(sDV, 1)

            @blk.scalar
            def _(sc):
                for i in range(NT):
                    if i % 2 == 0:
                        sc.wait_ge(qS, qs_x0e(i))
                    else:
                        sc.wait_ge(qG, qg_x0[i])
                    sc.activation(QKVR[:, 0:4096], x0st(i), AF.Square,
                                  accum_out=st(i, 1)).then_inc(sAC, 1)
                    sc.wait_ge(sDV, dv_pre(i))
                    sc.activation(st(i, 5), st(i, 6), AF.Ln, bias=epsc[:, :]).then_inc(sAC, 1)
                    sc.wait_ge(sAC, ac_ln(i))
                    sc.activation(st(i, 5), st(i, 5), AF.Exp, scale=-0.5).then_inc(sAC, 1)
                    sc.wait_ge(sDV, dv_apply(i))
                    dst = bass.AP(XLNR, 16 + 128 * i, [[XLNR_COLS, 128], [XP, 32], [1, 128]])
                    sc.dma_start(dst, xt_st(i), transpose=True).then_inc(qT, 16)
                x0h = A32[0:2, 0:DM]
                xth16 = A16[0:16, 16384:16384 + DM]
                sc.wait_ge(qS, qs_p1_end)
                sc.activation(QKVR[0:2, 0:4096], x0h, AF.Square,
                              accum_out=sth(1)).then_inc(sAC, 1)
                sc.wait_ge(sDV, dv_pre_h)
                sc.activation(sth(5), sth(6), AF.Ln, bias=epsc[0:2, :]).then_inc(sAC, 1)
                sc.wait_ge(sAC, ac_ln_h)
                sc.activation(sth(5), sth(5), AF.Exp, scale=-0.5).then_inc(sAC, 1)
                sc.wait_ge(sDV, dv_apply_h)
                sc.wait_ge(qT, qt_tp1(7))
                dsth = bass.AP(A16, 20480, [[24576, 128], [128, 32], [1, 128]])
                sc.dma_start(dsth, A16[:, 16384:16384 + DM], transpose=True).then_inc(qT, 16)

        if "t_xln1" in tap_out:
            _dump(tap_out["t_xln1"], XLNR[:, :])
        if "t_stage" in tap_out:
            _dump(tap_out["t_stage"], A16[:, 0:24576])
        if "t_stt" in tap_out:
            _dump(tap_out["t_stt"], stt[:, :])
        if "t_x0" in tap_out:
            _dump(tap_out["t_x0"], A32[:, 0:8192])

        if last_phase < 2:
            return nc
        pe_slab = lambda j: (j + 1) + (j // 24) + 1
        pe_stop = lambda g: 25 * (g + 1)
        dv2_evac = lambda g: dv_halo + 4 * (g + 1)
        ac2_0 = 3 * NT + 3
        ac2_evac = lambda g: ac2_0 + 4 * (g + 1)
        wqkv_val = lambda x: slab_val(143) + 16 * (x + 1)

        with nc.Block("p2", no_gpsimd_drain=True) as blk:
            @blk.tensor
            def _(t):
                t.wait_ge(qT, qt_tp1(7))
                t.wait_ge(sDV, dv_halo)
                j = 0
                for g in range(6):
                    if g >= 1:
                        t.wait_ge(sDV, dv2_evac(g - 1))
                        t.wait_ge(sAC, ac2_evac(g - 1))
                    for mt in range(4):
                        lhsT = ATTNW[0:1, (4 * g + mt) * 128:(4 * g + mt) * 128 + 128]
                        for th in range(2):
                            ins = t.matmul(bank(mt * 2 + th), lhsT, ones_r[0:1, 0:512],
                                           start=True, stop=False)
                    ins.then_inc(sPE, 1)
                    for w in range(3):
                        for k4 in range(8):
                            t.wait_ge(qG, slab_val(j))
                            slab = slab_buf(j)
                            last = (w == 2 and k4 == 7)
                            for kk in range(4):
                                k = k4 * 4 + kk
                                for mt in range(4):
                                    lhsT = slab[:, kk * 512 + mt * 128: kk * 512 + mt * 128 + 128]
                                    for th in range(2):
                                        ins = t.matmul(bank(mt * 2 + th), lhsT, xln_rhs(k, w, th),
                                                       start=False, stop=(last and kk == 3))
                            ins.then_inc(sPE, 1)
                            j += 1

            @blk.gpsimd
            def _(g_):
                for j in range(2, 144):
                    gg, rem = divmod(j, 24)
                    w_, k4_ = divmod(rem, 8)
                    if j >= 4:
                        g_.wait_ge(sPE, pe_slab(j - 4))
                    src = bass.AP(convw_d.tensor,
                                  w_ * DM * FOUT + (k4_ * 512) * FOUT + gg * 512,
                                  [[FOUT, 128], [128 * FOUT, 4], [1, 512]])
                    dst = ap3(slab_buf(j).tensor, slab_buf(j).offset, [[512, 4], [1, 512]])
                    g_.dma_start(dst, src).then_inc(qG, 16)
                g_.wait_ge(sDV, dv_halo)
                g_.wait_ge(sPE, pe_slab(143))
                for k in range(8):
                    g_.dma_start(A16[:, k * 1024:(k + 1) * 1024],
                                 wq_d[k * 128:(k + 1) * 128, :]).then_inc(qG, 16)
                for k in range(8):
                    g_.dma_start(A16[:, 8192 + k * 1024: 8192 + (k + 1) * 1024],
                                 wk_d[k * 128:(k + 1) * 128, :]).then_inc(qG, 16)
                for k in range(8):
                    g_.dma_start(A16[:, 16384 + k * 1024: 16384 + (k + 1) * 1024],
                                 wv_d[k * 128:(k + 1) * 128, :]).then_inc(qG, 16)

            @blk.vector
            def _(v):
                for g in range(6):
                    v.wait_ge(sPE, pe_stop(g))
                    for b in range(4, 8):
                        mt, th = divmod(b, 2)
                        v.tensor_copy(QKVR[:, (4 * g + mt) * 1024 + th * 512:
                                            (4 * g + mt) * 1024 + th * 512 + 512],
                                      bank(b)).then_inc(sDV, 1)

            @blk.scalar
            def _(sc):
                for g in range(6):
                    sc.wait_ge(sPE, pe_stop(g))
                    for b in range(0, 4):
                        mt, th = divmod(b, 2)
                        sc.copy(QKVR[:, (4 * g + mt) * 1024 + th * 512:
                                      (4 * g + mt) * 1024 + th * 512 + 512],
                                bank(b)).then_inc(sAC, 1)

        if "t_conv" in tap_out:
            _dump(tap_out["t_conv"], QKVR[:, :])

        if last_phase < 3:
            return nc
        pe3_0 = pe_stop(5)
        pe3 = lambda G: pe3_0 + G + 1
        dv3_0 = dv2_evac(5)
        dv3 = lambda G: dv3_0 + G + 1
        ac3_0 = ac2_evac(5)
        wo_val = lambda k: wqkv_val(23) + 16 * (k + 1)

        with nc.Block("p3", no_gpsimd_drain=True) as blk:
            @blk.tensor
            def _(t):
                G = 0
                for kind in range(2):
                    t.wait_ge(qG, wqkv_val(7 + 8 * kind))
                    for m in range(8):
                        for th in range(2):
                            if G >= 4:
                                t.wait_ge(sDV, dv3(G - 4))
                            bk_ = G % 4
                            for k in range(8):
                                ins = t.matmul(
                                    bank(bk_),
                                    A16[:, kind * 8192 + k * 1024 + m * 128:
                                        kind * 8192 + k * 1024 + m * 128 + 128],
                                    QKVR[:, (kind * 8 + k) * 1024 + th * 512:
                                         (kind * 8 + k) * 1024 + th * 512 + 512],
                                    start=(k == 0), stop=(k == 7))
                            ins.then_inc(sPE, 1)
                            G += 1
                t.wait_ge(qG, wqkv_val(23))
                for tt in range(8):
                    for hh in range(2):
                        if G >= 4:
                            t.wait_ge(sDV, dv3(G - 4))
                        bk_ = G % 4
                        for k in range(8):
                            t.matmul(bank(bk_),
                                     QKVR[:, (16 + k) * 1024 + tt * 128:
                                          (16 + k) * 1024 + tt * 128 + 128],
                                     A16[:, 16384 + k * 1024 + hh * 512:
                                         16384 + k * 1024 + hh * 512 + 512],
                                     start=(k == 0), stop=False)
                        ins = t.matmul(bank(bk_), ones_r[0:1, 0:128],
                                       ATTNW[0:1, FOUT + hh * 512: FOUT + hh * 512 + 512],
                                       start=False, stop=True)
                        ins.then_inc(sPE, 1)
                        G += 1

            @blk.gpsimd
            def _(g_):
                g_.wait_ge(sPE, pe3(15))
                for k in range(8):
                    g_.dma_start(A16[:, k * 1024:(k + 1) * 1024],
                                 wo_d[k * 128:(k + 1) * 128, :]).then_inc(qG, 16)

            @blk.vector
            def _(v):
                G = 0
                for kind in range(2):
                    for m in range(8):
                        for th in range(2):
                            v.wait_ge(sPE, pe3(G))
                            dst = XLNR[:, kind * 8192 + m * 1024 + th * 512:
                                       kind * 8192 + m * 1024 + th * 512 + 512]
                            if kind == 0:
                                v.tensor_scalar(dst, bank(G % 4), bq2[:, m:m + 1], 0.125,
                                                ALU.add, ALU.mult).then_inc(sDV, 1)
                            else:
                                v.tensor_scalar(dst, bank(G % 4), bk2[:, m:m + 1], None,
                                                ALU.add).then_inc(sDV, 1)
                            G += 1
                for tt in range(8):
                    for hh in range(2):
                        v.wait_ge(sPE, pe3(G))
                        v.tensor_copy(XLNR[:, 16384 + tt * 1024 + hh * 512:
                                           16384 + tt * 1024 + hh * 512 + 512],
                                      bank(G % 4)).then_inc(sDV, 1)
                        G += 1

        if "t_qkvh" in tap_out:
            _dump(tap_out["t_qkvh"], XLNR[:, 0:24576])

        if last_phase < 4:
            return nc
        pe4_0 = pe3(47)
        dv4_0 = dv3(47)
        ac4_0 = ac3_0
        qt4_0 = qt_tp1h

        order = []
        for b in range(8):
            order.append(("sc", b))
            if b >= 1:
                order.append(("pv", b - 1))
                if b - 1 == 3:
                    order.append(("wo", 0))
        order.append(("pv", 7))
        order.append(("wo", 1))

        pe_sc = {}; pe_pv = {}; pe_wo = {}
        cnt = pe4_0
        for kind, idx in order:
            if kind == "sc":
                for hg in range(4):
                    cnt += 1
                    pe_sc[(idx, hg)] = cnt
            elif kind == "pv":
                for j in range(8):
                    cnt += 1
                    pe_pv[(idx, j)] = cnt
            else:
                for wt in range(8):
                    cnt += 1
                    pe_wo[(idx, wt)] = cnt
        pe4_end = cnt

        dorder = []
        for b in range(8):
            dorder.append(("sm", b))
            if b >= 1:
                dorder.append(("oe", b - 1))
                if b - 1 == 3:
                    dorder.append(("we", 0))
        dorder.append(("oe", 7))
        dorder.append(("we", 1))

        dv_ma = {}; dv_rm = {}; dv_pn = {}; dv_oe = {}; dv_we = {}
        cnt = dv4_0
        for kind, idx in dorder:
            if kind == "sm":
                for hg in range(4):
                    cnt += 1; dv_ma[(idx, hg)] = cnt
                for hg in range(4):
                    cnt += 1; dv_pn[(idx, hg, "ss")] = cnt
                    cnt += 1; dv_pn[(idx, hg, "r")] = cnt
                    cnt += 1; dv_pn[(idx, hg)] = cnt
            elif kind == "oe":
                for j in range(8):
                    cnt += 1
                    dv_oe[(idx, j)] = cnt
            else:
                for wt in range(8):
                    cnt += 1
                    dv_we[(idx, wt)] = cnt
        dv4_end = cnt

        ac_exp = {}
        cnt = ac4_0
        for b in range(8):
            for hg in range(4):
                cnt += 1
                ac_exp[(b, hg)] = cnt
        ac4_end = cnt
        qt_pt = lambda b: qt4_0 + 16 * (b + 1)

        def qh_ap(h, b):
            return XLNR[(h % 2) * 64:(h % 2) * 64 + 64,
                        (h // 2) * 1024 + b * 128:(h // 2) * 1024 + b * 128 + 128]

        def kh_ap(h, b):
            return XLNR[(h % 2) * 64:(h % 2) * 64 + 64,
                        8192 + (h // 2) * 1024 + b * 128: 8192 + (h // 2) * 1024 + b * 128 + 128]

        def vh_ap(h, b):
            return XLNR[:, 16384 + b * 1024 + h * 64: 16384 + b * 1024 + h * 64 + 64]

        def pall(b, h0, wd):
            return QKVR[:, 8192 + (b % 4) * 2048 + h0 * 128:
                        8192 + (b % 4) * 2048 + h0 * 128 + wd]

        def ptall(b, h):
            return QKVR[:, 16384 + (b % 4) * 2048 + h * 128:
                        16384 + (b % 4) * 2048 + h * 128 + 128]

        def ofm_col(ch, b):
            return ch * 1024 + (b // 4) * 512 + (b % 4) * 128

        x0_val = {}

        with nc.Block("p4", no_gpsimd_drain=True) as blk:
            @blk.gpsimd
            def _(g_):
                cnt = wo_val(7)
                for t_ in range(4):
                    g_.dma_start(x0b(t_), x_in[1 + 128 * t_: 129 + 128 * t_, :]).then_inc(qG, 16)
                    cnt += 16
                    x0_val[t_] = cnt

            @blk.tensor
            def _(t):
                for kind, idx in order:
                    if kind == "sc":
                        b = idx
                        for hg in range(4):
                            if b > 0 or hg > 0:
                                pb, phg = (b, hg - 1) if hg > 0 else (b - 1, 3)
                                t.wait_ge(sDV, dv_ma[(pb, phg)])
                            for h in range(hg * 4, hg * 4 + 4):
                                ins = t.matmul(bank(h % 4, 128), qh_ap(h, b), kh_ap(h, b),
                                               start=True, stop=True)
                            ins.then_inc(sPE, 1)
                    elif kind == "pv":
                        b = idx
                        t.wait_ge(qT, qt_pt(b))
                        for j in range(8):
                            bA = 4 + 2 * (j % 2)
                            if j >= 2:
                                t.wait_ge(sDV, dv_oe[(b, j - 2)])
                            elif b >= 1:
                                t.wait_ge(sDV, dv_oe[(b - 1, j + 6)])
                            if b == 4 and j < 2:
                                t.wait_ge(sDV, dv_we[(0, 6 + j)])
                            t.matmul(bank(bA, 128, p0=0, p1=64), vh_ap(2 * j, b),
                                     ptall(b, 2 * j), start=True, stop=True)
                            ins = t.matmul(bank(bA + 1, 128, p0=64, p1=128), vh_ap(2 * j + 1, b),
                                           ptall(b, 2 * j + 1), start=True, stop=True)
                            ins.then_inc(sPE, 1)
                    else:
                        g = idx
                        t.wait_ge(qG, wo_val(7))
                        for wt in range(8):
                            bk_ = 4 + (wt % 2)
                            if wt >= 2:
                                t.wait_ge(sDV, dv_we[(g, wt - 2)])
                            else:
                                t.wait_ge(sDV, dv_oe[(3 if g == 0 else 7, 6 + wt)])
                            for k in range(8):
                                ins = t.matmul(bank(bk_),
                                               A16[:, k * 1024 + wt * 128: k * 1024 + wt * 128 + 128],
                                               QKVR[:, k * 1024 + g * 512: k * 1024 + g * 512 + 512],
                                               start=(k == 0), stop=(k == 7))
                            ins.then_inc(sPE, 1)

            @blk.vector
            def _(v):
                for kind, idx in dorder:
                    if kind == "sm":
                        b = idx
                        for hg in range(4):
                            v.wait_ge(sPE, pe_sc[(b, hg)])
                            gi = b * 4 + hg
                            if gi >= 2:
                                pb, phg = divmod(gi - 2, 4)
                                v.wait_ge(sAC, ac_exp[(pb, phg)])
                            sl = (hg % 2) * 512
                            src = bass.AP(PS, 0, [[4096, 128], [512, 4], [1, 128]])
                            dst = bass.AP(s_m4, sl, [[1024, 128], [128, 4], [1, 128]])
                            v.tensor_tensor(dst, src,
                                            bass.AP(mask4, 0, [[512, 128], [128, 4], [1, 128]]),
                                            ALU.add).then_inc(sDV, 1)
                        for hg in range(4):
                            v.wait_ge(sAC, ac_exp[(b, hg)])
                            v.tensor_reduce(
                                bass.AP(sums4, (hg % 2) * 4, [[8, 128], [1, 4], [1, 1]]),
                                bass.AP(QKVR, 8192 + (b % 4) * 2048 + hg * 512,
                                        [[24576, 128], [128, 4], [1, 128]]),
                                AX, ALU.add).then_inc(sDV, 1)
                            v.wait_ge(sDV, dv_pn[(b, hg, "ss")])
                            v.reciprocal(rsum4[:, (hg % 2) * 4:(hg % 2) * 4 + 4],
                                         sums4[:, (hg % 2) * 4:(hg % 2) * 4 + 4]).then_inc(sDV, 1)
                            v.wait_ge(sDV, dv_pn[(b, hg, "r")])
                            for h in range(hg * 4, hg * 4 + 4):
                                ins = v.tensor_scalar(pall(b, h, 128), pall(b, h, 128),
                                                      rsum4[:, (hg % 2) * 4 + h % 4:
                                                            (hg % 2) * 4 + h % 4 + 1],
                                                      None, ALU.mult)
                            ins.then_inc(sDV, 1)
                    elif kind == "oe":
                        b = idx
                        for j in range(8):
                            bA = 4 + 2 * (j % 2)
                            v.wait_ge(sPE, pe_pv[(b, j)])
                            c0 = ofm_col(j, b)
                            v.tensor_copy(QKVR[0:64, c0:c0 + 128],
                                          bank(bA, 128, p0=0, p1=64))
                            v.tensor_copy(QKVR[64:128, c0:c0 + 128],
                                          bank(bA + 1, 128, p0=64, p1=128)).then_inc(sDV, 1)
                    else:
                        g = idx
                        for wt in range(8):
                            v.wait_ge(sPE, pe_wo[(g, wt)])
                            v.tensor_scalar(ATTNW[:, wt * 1024 + g * 512: wt * 1024 + g * 512 + 512],
                                            bank(4 + (wt % 2)), bo2[:, wt:wt + 1], None,
                                            ALU.add).then_inc(sDV, 1)

            @blk.scalar
            def _(sc):
                for b in range(8):
                    for hg in range(4):
                        gi = b * 4 + hg
                        sc.wait_ge(sDV, dv_ma[(b, hg)])
                        if b >= 4 and hg == 0:
                            sc.wait_ge(qT, qt_pt(b - 4))
                        sc.activation(pall(b, hg * 4, 512),
                                      s_m4[:, (hg % 2) * 512:(hg % 2) * 512 + 512],
                                      AF.Exp, bias=neg16[:, :]).then_inc(sAC, 1)
                    sc.wait_ge(sDV, dv_pn[(b, 3)])
                    if b >= 4:
                        sc.wait_ge(sPE, pe_pv[(b - 4, 7)])
                    dst = ap3(QKVR, 16384 + (b % 4) * 2048, [[128, 16], [1, 128]])
                    sc.dma_start(dst, pall(b, 0, 2048), transpose=True).then_inc(qT, 16)

        if "t_attnw" in tap_out:
            _dump(tap_out["t_attnw"], ATTNW[:, :])
        if "t_ofm" in tap_out:
            _dump(tap_out["t_ofm"], QKVR[:, 0:8192])
        if "t_pall" in tap_out:
            _dump(tap_out["t_pall"], QKVR[:, 8192:16384])
        if "t_sm" in tap_out:
            _dump(tap_out["t_sm"], s_m4[:, :])
            _dump(tap_out["t_sm2"], negmax4[:, :])
            _dump(tap_out["t_sm3"], sums4[:, :])
            _dump(tap_out["t_sm5"], rsum4[:, :])

        if last_phase < 5:
            return nc
        pe5_0 = pe4_end
        dv5_0 = dv4_end
        qg5_0 = wo_val(7) + 4 * 16
        projb_val = qg5_0 + 3 * 16
        projw_val = lambda gs: x0_val[("pw", gs)]
        pe_p5 = lambda gs: pe5_0 + gs + 1
        dv_p5 = lambda gs: dv5_0 + gs + 1 + 4 * (gs // 8)
        dv5_add = lambda t_: dv5_0 + 12 * (t_ // 4) + 9 + (t_ % 4)

        with nc.Block("p5", no_gpsimd_drain=True) as blk:
            @blk.gpsimd
            def _(g_):
                cnt = [qg5_0]
                for t_ in (4, 5):
                    g_.dma_start(x0b(t_), x_in[1 + 128 * t_: 129 + 128 * t_, :]).then_inc(qG, 16)
                    cnt[0] += 16
                    x0_val[t_] = cnt[0]
                g_.dma_start(QKVR[0:1, 0:DM], projb_d[None, :]).then_inc(qG, 16)
                cnt[0] += 16
                projb_val2 = cnt[0]
                for g in range(2):
                    for s in range(8):
                        gs = g * 8 + s
                        if gs >= 3:
                            g_.wait_ge(sPE, pe_p5(gs - 3))
                        src = bass.AP(projw_d.tensor, s * 512,
                                      [[DM, 128], [128 * DM, 8], [1, 512]])
                        dst = ap3(QKVR, 4096 + (gs % 3) * 4096, [[512, 8], [1, 512]])
                        g_.dma_start(dst, src).then_inc(qG, 16)
                        cnt[0] += 16
                        x0_val[("pw", gs)] = cnt[0]
                        if gs == 9:
                            g_.wait_ge(sDV, dv5_add(0))
                            g_.dma_start(x0b(6), x_in[1 + 128 * 6: 129 + 128 * 6, :]).then_inc(qG, 16)
                            cnt[0] += 16
                            x0_val[6] = cnt[0]
                        if gs == 10:
                            g_.wait_ge(sDV, dv5_add(1))
                            g_.dma_start(x0b(7), x_in[1 + 128 * 7: 129 + 128 * 7, :]).then_inc(qG, 16)
                            cnt[0] += 16
                            x0_val[7] = cnt[0]

            @blk.tensor
            def _(t):
                t.wait_ge(qG, projb_val)
                for g in range(2):
                    for s in range(8):
                        gs = g * 8 + s
                        if gs >= 2:
                            t.wait_ge(sDV, dv_p5(gs - 2))
                        t.wait_ge(qG, projw_val(gs))
                        for k in range(8):
                            rhs = QKVR[:, 4096 + (gs % 3) * 4096 + k * 512:
                                       4096 + (gs % 3) * 4096 + k * 512 + 512]
                            for tb in range(4):
                                t.matmul(bank(tb + 4 * (s % 2)),
                                         ATTNW[:, k * 1024 + g * 512 + tb * 128:
                                               k * 1024 + g * 512 + tb * 128 + 128],
                                         rhs, start=(k == 0), stop=False)
                        for tb in range(4):
                            ins = t.matmul(bank(tb + 4 * (s % 2)), ones_r[0:1, 0:128],
                                           QKVR[0:1, s * 512: s * 512 + 512],
                                           start=False, stop=True)
                        ins.then_inc(sPE, 1)

            @blk.vector
            def _(v):
                for g in range(2):
                    for s in range(8):
                        gs = g * 8 + s
                        v.wait_ge(sPE, pe_p5(gs))
                        for tb in range(4):
                            ins = v.tensor_copy(x1v(g * 4 + tb, s * 512, 512),
                                                bank(tb + 4 * (s % 2)))
                        ins.then_inc(sDV, 1)
                        if s == 7:
                            v.wait_ge(sDV, dv_p5(g * 8 + 7))
                            for tb in range(4):
                                t_ = g * 4 + tb
                                v.wait_ge(qG, x0_val[t_])
                                v.tensor_tensor(x1v(t_, 0, 4096), x1v(t_, 0, 4096),
                                                x0b(t_), ALU.add).then_inc(sDV, 1)

        if "t_x1a" in tap_out:
            _dump(tap_out["t_x1a"], XLNR[:, 0:32768])

        if last_phase < 6:
            return nc
        dv6_0 = dv5_add(7)
        ac6_0 = ac4_end
        qs6_0 = qs_p1_end
        qt6_0 = qt_pt(7)
        dv6_sum = lambda t_: dv6_0 + 5 * t_ + 1
        dv6_nm = lambda t_: dv6_0 + 5 * t_ + 2
        dv6_t1 = lambda t_: dv6_0 + 5 * t_ + 3
        dv6_pre = lambda t_: dv6_0 + 5 * t_ + 4
        dv6_ln = lambda t_: dv6_0 + 5 * t_ + 5
        ac6_sq = lambda t_: ac6_0 + 3 * t_ + 1
        ac6_ln = lambda t_: ac6_0 + 3 * t_ + 2
        ac6_rstd = lambda t_: ac6_0 + 3 * t_ + 3
        qs6_0b = qs_p1_end
        qs6_st = lambda t_: qs6_0b + 16 * (t_ + 1)
        qt6_tp = lambda t_: qt6_0 + 16 * (t_ + 1)

        def xln2tm(t_):
            return QKVR[:, (t_ % 2) * 4096:(t_ % 2) * 4096 + 4096]

        with nc.Block("p6", no_gpsimd_drain=True) as blk:
            @blk.sync
            def _(s):
                for t_ in range(n6):
                    s.wait_ge(qT, qt6_tp(t_))
                    dst = bass.AP(xln2_d.tensor, 128 * t_,
                                  [[T, 128], [128 * T, 32], [1, 128]])
                    s.dma_start(dst, ap3(QKVR, 8192 + (t_ % 2) * 4096, [[128, 32], [1, 128]])
                                ).then_inc(qS, 16)

            @blk.vector
            def _(v):
                for t_ in range(n6):
                    v.reduce_sum(st(t_, 0), x1v(t_, 0, 4096), axis=AX).then_inc(sDV, 1)
                    v.wait_ge(sDV, dv6_sum(t_))
                    v.tensor_scalar(st(t_, 2), st(t_, 0), -1.0 / DM, None, ALU.mult).then_inc(sDV, 1)
                    v.wait_ge(sAC, ac6_sq(t_))
                    v.wait_ge(sDV, dv6_nm(t_))
                    v.tensor_tensor(st(t_, 3), st(t_, 2), st(t_, 2), ALU.mult)
                    v.tensor_scalar(st(t_, 4), st(t_, 1), 1.0 / DM, None,
                                    ALU.mult).then_inc(sDV, 1)
                    v.wait_ge(sDV, dv6_t1(t_))
                    v.tensor_tensor(st(t_, 6), st(t_, 4), st(t_, 3),
                                    ALU.subtract).then_inc(sDV, 1)
                    v.wait_ge(sAC, ac6_rstd(t_))
                    v.tensor_scalar(xln2tm(t_), x1v(t_, 0, 4096), st(t_, 2), st(t_, 5),
                                    ALU.add, ALU.mult).then_inc(sDV, 1)

            @blk.scalar
            def _(sc):
                for t_ in range(n6):
                    if t_ >= 2:
                        sc.wait_ge(qT, qt6_tp(t_ - 2))
                    sc.activation(xln2tm(t_), x1v(t_, 0, 4096), AF.Square,
                                  accum_out=st(t_, 1)).then_inc(sAC, 1)
                    sc.wait_ge(sDV, dv6_pre(t_))
                    sc.activation(st(t_, 5), st(t_, 6), AF.Ln, bias=epsc[:, :]).then_inc(sAC, 1)
                    sc.wait_ge(sAC, ac6_ln(t_))
                    sc.activation(st(t_, 5), st(t_, 5), AF.Exp, scale=-0.5).then_inc(sAC, 1)
                    sc.wait_ge(sDV, dv6_ln(t_))
                    if t_ >= 2:
                        sc.wait_ge(qS, qs6_st(t_ - 2))
                    if "notp" in p6dbg:
                        sc.dma_start(QKVR[:, 8192 + (t_ % 2) * 4096:
                                          8192 + (t_ % 2) * 4096 + 4096],
                                     xln2tm(t_)).then_inc(qT, 16)
                    else:
                        dst = ap3(QKVR, 8192 + (t_ % 2) * 4096, [[128, 32], [1, 128]])
                        sc.dma_start(dst, xln2tm(t_), transpose=True).then_inc(qT, 16)

            @blk.gpsimd
            def _(g_):
                g_.dma_start(ap3(SLAB, 0, [[1024, 2], [1, 1024]]),
                             bass.AP(up_d.tensor, 0, [[MID, 128], [128 * MID, 2], [1, 1024]])
                             ).then_inc(qG, 16)
                g_.dma_start(ap3(SLAB, 2048, [[256, 8], [1, 256]]),
                             bass.AP(down_d.tensor, 0, [[256, 128], [128 * 256, 8], [1, 256]])
                             ).then_inc(qG, 16)

        if "t_x1" in tap_out:
            _dump(tap_out["t_x1"], XLNR[:, 0:32768])
        if "t_xln2" in tap_out:
            _dump_dram(tap_out["t_xln2"], xln2_d)

        if last_phase < 7:
            return nc
        qg7_0 = projw_val(15) + 2 * 16
        pe7_0 = pe_p5(15)
        dv7_0 = dv6_ln(7)
        ac7_0 = ac6_rstd(7)
        qs7_end = qs6_st(n6 - 1)

        qg7_rec = {}

        def qg7_gat(h):
            return qg7_rec[("gat", h)]

        def qg7_w(h):
            return qg7_rec[("w", h)]

        def qg7_scat(h, dt):
            return qg7_rec[("scat", h, dt)]

        pe_up7 = lambda h, u2: pe7_0 + 8 * h + u2 + 1
        pe_dn7 = lambda h, i: pe7_0 + 8 * h + 4 + i + 1
        ac_g7 = lambda h, u2, th: ac7_0 + 8 * h + u2 * 2 + th + 1
        dv_e7 = lambda h, i: dv7_0 + 4 * h + i + 1

        def xp(h, c):
            return QKVR[:, 16384 + c * 2048 + (h % 2) * 1024:
                        16384 + c * 2048 + (h % 2) * 1024 + 1024]

        def upb(h, c):
            base = SLAB if h % 2 == 0 else QKVR
            return base[:, c * 1024: c * 1024 + 1024]

        def dnb(h):
            base = SLAB if h % 2 == 0 else QKVR
            return base[:, 2048:4096]

        def midv(th, c0, wd):
            return ATTNW[:, th * 4096 + c0: th * 4096 + c0 + wd]

        def dout_full(h, dt):
            return QKVR[:, 20480 + dt * 2048 + (h % 2) * 1024:
                        20480 + dt * 2048 + (h % 2) * 1024 + 1024]

        def dout(h, dt, th):
            c = 20480 + dt * 2048 + (h % 2) * 1024 + th * 512
            return QKVR[:, c: c + 512]

        with nc.Block("p7", no_gpsimd_drain=True) as blk:
            @blk.gpsimd
            def _(g_):
                cnt = [qg7_0]

                def dma(instr):
                    cnt[0] += 16
                    instr.then_inc(qG, 16)

                def emit_scat(h):
                    for dt in range(2):
                        g_.wait_ge(sDV, dv_e7(h, dt * 2 + 1))
                        dma(g_.indirect_dma_start(
                            scat_d, IndirectOffsetOnAxis(perm2[:, 2 * h + dt: 2 * h + dt + 1], 0),
                            dout_full(h, dt), None))
                        qg7_rec[("scat", h, dt)] = cnt[0]

                g_.wait_ge(qS, qs7_end)
                g_.wait_ge(qT, qt6_tp(7))
                qg7_rec[("w", 0)] = qg7_0
                for h in range(16):
                    if h >= 2:
                        g_.wait_ge(sPE, pe_up7(h - 2, 3))
                    for c in range(2):
                        dma(g_.indirect_dma_start(
                            xp(h, c), None, xln2_d,
                            IndirectOffsetOnAxis(perm2[:, 2 * h + c: 2 * h + c + 1], 0)))
                    qg7_rec[("gat", h)] = cnt[0]
                    if h >= 1:
                        if h >= 2:
                            g_.wait_ge(sPE, pe_dn7(h - 2, 3))
                        base = SLAB if h % 2 == 0 else QKVR
                        dma(g_.dma_start(ap3(base, 0, [[1024, 2], [1, 1024]]),
                                         bass.AP(up_d.tensor, h * 256 * MID,
                                                 [[MID, 128], [128 * MID, 2], [1, 1024]])))
                        dma(g_.dma_start(ap3(base, 2048, [[256, 8], [1, 256]]),
                                         bass.AP(down_d.tensor, h * MID * 256,
                                                 [[256, 128], [128 * 256, 8], [1, 256]])))
                        qg7_rec[("w", h)] = cnt[0]
                    if h >= 2:
                        emit_scat(h - 2)
                emit_scat(14)
                emit_scat(15)

            @blk.tensor
            def _(t):
                for h in range(16):
                    t.wait_ge(qG, qg7_gat(h))
                    t.wait_ge(qG, qg7_w(h))
                    for u2 in range(4):
                        B = (u2 % 2) * 4
                        if u2 >= 2:
                            t.wait_ge(sAC, ac_g7(h, u2 - 2, 1))
                        elif h >= 1:
                            if u2 == 0:
                                t.wait_ge(sDV, dv_e7(h - 1, 3))
                            else:
                                t.wait_ge(sAC, ac_g7(h - 1, 3, 1))
                        for mq in range(2):
                            m = u2 * 2 + mq
                            for kd in range(2):
                                lhsT = upb(h, kd)[:, m * 128: m * 128 + 128]
                                for th in range(2):
                                    ins = t.matmul(bank(B + mq * 2 + th),
                                                   lhsT, xp(h, kd)[:, th * 512: th * 512 + 512],
                                                   start=(kd == 0), stop=(kd == 1))
                        ins.then_inc(sPE, 1)
                    for half in range(2):
                        t.wait_ge(sAC, ac_g7(h, 2 + half, 1))
                        for kd in range(4 * half, 4 * half + 4):
                            for dt in range(2):
                                lhsT = dnb(h)[:, kd * 256 + dt * 128: kd * 256 + dt * 128 + 128]
                                for th in range(2):
                                    ins = t.matmul(bank(dt * 2 + th), lhsT,
                                                   midv(th, kd * 512, 512),
                                                   start=(kd == 0), stop=(kd == 7))
                                    if kd == 7:
                                        ins.then_inc(sPE, 1)

            @blk.scalar
            def _(sc):
                for h in range(16):
                    for u2 in range(4):
                        B = (u2 % 2) * 4
                        sc.wait_ge(sPE, pe_up7(h, u2))
                        if h >= 1:
                            sc.wait_ge(sPE, pe_dn7(h - 1, 3))
                        for th in range(2):
                            src = bass.AP(PS, B * 512 + th * 512, [[4096, 128], [1024, 2], [1, 512]])
                            sc.activation(midv(th, u2 * 1024, 1024), src,
                                          AF.Gelu).then_inc(sAC, 1)

            @blk.vector
            def _(v):
                for h in range(16):
                    for dt in range(2):
                        for th in range(2):
                            v.wait_ge(sPE, pe_dn7(h, dt * 2 + th))
                            if h >= 2:
                                v.wait_ge(qG, qg7_rec[("scat", h - 2, dt)])
                            v.tensor_copy(dout(h, dt, th),
                                          bank(dt * 2 + th)).then_inc(sDV, 1)

        if "t_scat" in tap_out:
            _dump_dram(tap_out["t_scat"], scat_d)

        if last_phase < 8:
            return nc
        qg8_end = qg7_scat(15, 1)
        qt8_0 = qt6_tp(7)
        dv8_0 = dv_e7(15, 3)
        qs8_0 = qs7_end
        qt8_tp = lambda t_: qt8_0 + 16 * (t_ + 1)
        dv8_add = lambda t_: dv8_0 + t_ + 1
        qs8_st = lambda t_: qs8_0 + 16 * (t_ + 1)

        def scat_tm(t_):
            return A16[:, 16384 + (t_ % 2) * DM: 16384 + (t_ % 2) * DM + DM]

        def out_st(t_):
            return A32[:, (t_ % 2) * DM:(t_ % 2) * DM + DM]

        with nc.Block("p8", no_gpsimd_drain=True) as blk:
            @blk.scalar
            def _(sc):
                sc.wait_ge(qG, qg8_end)
                for t_ in range(NT):
                    if t_ >= 2:
                        sc.wait_ge(sDV, dv8_add(t_ - 2))
                    sc.dma_start(scat_tm(t_), scat_d[:, 128 * t_: 128 * (t_ + 1)],
                                 transpose=True).then_inc(qT, 16)

            @blk.sync
            def _(s):
                for t_ in range(0, NT, 2):
                    s.wait_ge(sDV, dv8_add(t_))
                    s.dma_start(out_d[128 * t_:128 * (t_ + 1), :], out_st(t_)).then_inc(qS, 16)
                s.wait_ge(qS, qs8_0 + 16 * 4)
                s.wait_ge(qG, qg8_end + 16 * 4)

            @blk.vector
            def _(v):
                for t_ in range(NT):
                    v.wait_ge(qT, qt8_0 + 16 * (t_ + 1))
                    if t_ >= 2:
                        tp_ = t_ - 2
                        if tp_ % 2 == 0:
                            v.wait_ge(qS, qs8_0 + 16 * (tp_ // 2 + 1))
                        else:
                            v.wait_ge(qG, qg8_end + 16 * (tp_ // 2 + 1))
                    v.tensor_tensor(out_st(t_), scat_tm(t_), x1v(t_, 0, 4096),
                                    ALU.add).then_inc(sDV, 1)

            @blk.gpsimd
            def _(g_):
                for t_ in range(1, NT, 2):
                    g_.wait_ge(sDV, dv8_add(t_))
                    g_.dma_start(out_d[128 * t_:128 * (t_ + 1), :], out_st(t_)).then_inc(qG, 16)

    return nc


def make_inputs(full):
    x = np.asarray(full["L0_attn_inputs"], dtype=np.float32)
    conv_w = np.ascontiguousarray(np.asarray(full["L0_attn_conv_w"], dtype=np.float32))
    conv_b = np.ascontiguousarray(np.asarray(full["L0_attn_conv_b"], dtype=np.float32))
    wq = np.ascontiguousarray(np.asarray(full["L0_attn_wq"], dtype=np.float32).reshape(W, W))
    wk = np.ascontiguousarray(np.asarray(full["L0_attn_wk"], dtype=np.float32).reshape(W, W))
    wv = np.ascontiguousarray(np.asarray(full["L0_attn_wv"], dtype=np.float32).reshape(W, W))
    bq = np.ascontiguousarray(np.asarray(full["L0_attn_bq"], dtype=np.float32).reshape(8, 128).T)
    bk = np.ascontiguousarray(np.asarray(full["L0_attn_bk"], dtype=np.float32).reshape(8, 128).T)
    bv = np.ascontiguousarray(np.asarray(full["L0_attn_bv"], dtype=np.float32).reshape(W))
    wo = np.ascontiguousarray(np.asarray(full["L0_attn_wo"], dtype=np.float32).reshape(W, W))
    bo = np.ascontiguousarray(np.asarray(full["L0_attn_bo"], dtype=np.float32).reshape(8, 128).T)
    projw = np.ascontiguousarray(np.asarray(full["L0_attn_proj_w"], dtype=np.float32))
    projb = np.ascontiguousarray(np.asarray(full["L0_attn_proj_b"], dtype=np.float32))
    up = np.ascontiguousarray(np.asarray(full["L0_ffn_up"], dtype=np.float32))
    down = np.ascontiguousarray(np.asarray(full["L0_ffn_down"], dtype=np.float32))
    perm = np.ascontiguousarray(
        np.asarray(full["L0_ffn_perm"]).reshape(32, 128).T).astype(np.uint32)

    tril = np.tril(np.ones((128, 128), np.float32))
    mask = np.where(tril > 0, 0.0, np.float32(-30000.0)).astype(np.float32)
    mask4 = np.ascontiguousarray(np.tile(mask, (1, 4)))

    shared = dict(convw=conv_w, convb=conv_b, wq=wq, wk=wk, wv=wv,
                  bq2d=bq, bk2d=bk, bv=bv, wo=wo, bo2d=bo,
                  projw=projw, projb=projb, ffup=up, ffdown=down,
                  perm2d=perm, mask4d=mask4)

    in_maps = []
    for core in range(8):
        b, q0 = divmod(core * T, 4096)
        lo = max(q0 - 1, 0)
        hi = min(q0 + T + 1, 4096)
        sl = np.asarray(x[b, lo:hi, :], dtype=np.float32)
        pad_lo = 1 - (q0 - lo)
        pad_hi = 1 - (hi - q0 - T)
        if pad_lo or pad_hi:
            sl = np.pad(sl, ((pad_lo, pad_hi), (0, 0)))
        m = dict(shared)
        m["x_in"] = np.ascontiguousarray(sl)
        in_maps.append(m)
    return in_maps


_NC_CACHE = {}


def kernel(**inputs):
    in_maps = make_inputs(inputs)
    if "nc" not in _NC_CACHE:
        _NC_CACHE["nc"] = build_nc()
    nc = _NC_CACHE["nc"]
    res = run_bass_kernel_spmd(nc, in_maps, core_ids=list(range(8)))
    outs = [np.asarray(res.results[c]["out"]) for c in range(8)]
    full = np.concatenate(outs, axis=0).reshape(2, 4096, DM)
    return full.astype(np.float32)
